# version 07
# baseline (speedup 1.0000x reference)
import math
from contextlib import ExitStack

import numpy as np
import concourse.bass as bass
import concourse.mybir as mybir
from concourse.bass_utils import run_bass_kernel_spmd

F32 = mybir.dt.float32
BF16 = mybir.dt.bfloat16
U8 = mybir.dt.uint8
AF = mybir.ActivationFunctionType
ALU = mybir.AluOpType
AX = mybir.AxisListType

NCORE = 8
B_, L_, D_ = 2, 4096, 2048
TPC = 1024
NCH = 16
NSL = 2048
EXPS = -20.0
E20 = float(np.exp(np.float32(20.0)))
NEG = -1.0e30
EPS = 1e-5
SCALE_A = 0.125
SCALE_B = float(1.0 / np.sqrt(np.float32(128.0)))

DEBUG_ATTN = False
SKIP = set()
ORDER = ["attn_a", "ffn1", "attn_b", "ffn2"]
DEBUG_DEQ = False

ENGINES = ("tensor", "vector", "scalar", "gpsimd", "sync")



def _rope_tables():
    inv = ((1.0 / 1024.0) ** np.linspace(0.0, 1.0, 16)).astype(np.float32)
    pos = np.empty(8192, np.float32)
    for r in range(8):
        pos[r * 1024:(r + 1) * 1024] = (r % 4) * 1024 + np.arange(1024, dtype=np.float32)
    cos = np.zeros((128, 8192), np.float32)
    sin = np.zeros((128, 8192), np.float32)
    for r in range(128):
        m = r % 64
        if m < 16:
            d, sgn = m, 1.0
        elif 32 <= m < 48:
            d, sgn = m - 32, -1.0
        else:
            continue
        th = (pos.astype(np.float32) * inv[d]).astype(np.float32)
        cos[r] = np.cos(th).astype(np.float32)
        sin[r] = (sgn * np.sin(th)).astype(np.float32)
    return cos, sin


def _toeplitz():
    i = np.arange(128)[:, None]
    w = np.arange(1664)[None, :]
    d = (w - 511) - i
    return np.where((d >= 0) & (d < 512), np.float32(NEG), np.float32(0.0)).astype(np.float32)


def band_js(qb):
    return [j for j in range(32) if -511 <= 512 * qb - 128 * j <= 638]


def stage_inputs(inputs):
    x = np.asarray(inputs["L0_attn_a_x"], np.float32)
    wqkv = np.asarray(inputs["L0_attn_a_W_qkv"], np.float32)
    wout = np.ascontiguousarray(np.asarray(inputs["L0_attn_a_W_out"], np.float32))
    s_sink = np.asarray(inputs["L0_attn_a_s"], np.float32)
    wq = np.asarray(inputs["L0_attn_b_Wq"], np.float32)
    wk = np.asarray(inputs["L0_attn_b_Wk"], np.float32)
    wv = np.asarray(inputs["L0_attn_b_Wv"], np.float32)
    wo = np.asarray(inputs["L0_attn_b_Wo"], np.float32)
    bq = np.asarray(inputs["L0_attn_b_bq"], np.float32)
    bk = np.asarray(inputs["L0_attn_b_bk"], np.float32)
    bv = np.asarray(inputs["L0_attn_b_bv"], np.float32)
    bo = np.asarray(inputs["L0_attn_b_bo"], np.float32)
    gpk = np.asarray(inputs["L0_ffn_gate_W_packed"], np.int32)
    gsc = np.asarray(inputs["L0_ffn_gate_scales"], np.float32)
    upk = np.asarray(inputs["L0_ffn_up_W_packed"], np.int32)
    usc = np.asarray(inputs["L0_ffn_up_scales"], np.float32)
    dpk = np.asarray(inputs["L0_ffn_down_W_packed"], np.int32)
    dsc = np.asarray(inputs["L0_ffn_down_scales"], np.float32)

    cos, sin = _rope_tables()
    toep = _toeplitz()
    ident = np.eye(128, dtype=np.float32)

    def u8view(a):
        return np.ascontiguousarray(np.ascontiguousarray(a).view(np.uint8)[:, 0::4])

    wqT = np.ascontiguousarray(wq.T)
    wkT = np.ascontiguousarray(wk.T)
    wvT = np.ascontiguousarray(wv.T)
    woT = np.ascontiguousarray(wo.T)

    in_maps = []
    for c in range(NCORE):
        bb, r = c // 4, c % 4
        m = {}
        m["x_in"] = np.ascontiguousarray(x[bb, r * TPC:(r + 1) * TPC, :])
        qcols = wqkv[:, c * 256:(c + 1) * 256]
        kcols = wqkv[:, 2048 + c * 64:2048 + (c + 1) * 64]
        vcols = wqkv[:, 2560 + c * 64:2560 + (c + 1) * 64]
        m["wqkv_sl"] = np.ascontiguousarray(np.concatenate([qcols, kcols, vcols], axis=1))
        m["wout_sl"] = np.ascontiguousarray(wout[c * 256:(c + 1) * 256, :])
        m["wq_sl"] = np.ascontiguousarray(wqT[:, 2 * c * 128:(2 * c + 2) * 128])
        m["wk_sl"] = np.ascontiguousarray(wkT[:, 2 * c * 128:(2 * c + 2) * 128])
        m["wv_sl"] = np.ascontiguousarray(wvT[:, 2 * c * 128:(2 * c + 2) * 128])
        m["wo_sl"] = np.ascontiguousarray(woT[c * 256:(c + 1) * 256, :])
        m["bq_sl"] = np.ascontiguousarray(bq[2 * c * 128:(2 * c + 2) * 128].reshape(2, 128))
        m["bk_sl"] = np.ascontiguousarray(bk[2 * c * 128:(2 * c + 2) * 128].reshape(2, 128))
        m["bv_sl"] = np.ascontiguousarray(bv[2 * c * 128:(2 * c + 2) * 128].reshape(2, 128))
        m["bo"] = np.ascontiguousarray(bo.reshape(16, 128))
        m["gpk"] = u8view(gpk[c * NSL:(c + 1) * NSL, :])
        m["gsc"] = np.ascontiguousarray(gsc[c * NSL:(c + 1) * NSL, :])
        m["upk"] = u8view(upk[c * NSL:(c + 1) * NSL, :])
        m["usc"] = np.ascontiguousarray(usc[c * NSL:(c + 1) * NSL, :])
        m["dpk"] = u8view(dpk[:, c * 1024:(c + 1) * 1024])
        m["dsc"] = np.ascontiguousarray(dsc[:, c * 64:(c + 1) * 64])
        m["cos_t"] = cos
        m["sin_t"] = sin
        m["toep"] = toep
        m["ident"] = ident
        m["sink"] = np.ascontiguousarray(s_sink[0, c, :].reshape(1, 4))
        in_maps.append(m)
    return in_maps



class Bld:
    def __init__(self, ctx):
        self.nc = bass.Bass("TRN2", target_bir_lowering=False, debug=False)
        self.ctx = ctx
        self.sems = {}
        self.cnt = {}
        self.phase_idx = 0
        self.block = None

    def sem(self, name):
        if name not in self.sems:
            self.sems[name] = self.ctx.enter_context(self.nc.semaphore(name))
            self.cnt[name] = 0
        return self.sems[name]

    def cc_wait(self, eng, at=None):
        v = self.cnt.get("s_cc", 0) if at is None else at
        if v > 0:
            eng.wait_ge(self.sem("s_cc"), v)


class Ch:

    def __init__(self, b, name, step=1):
        self.b = b
        self.name = name
        self.step = step
        self.semh = b.sem(name)
        self.base = b.cnt[name]

    def inc(self, inst, items=1):
        inst.then_inc(self.semh, self.step * items)
        self.b.cnt[self.name] += self.step * items
        return inst

    def wait(self, eng, items):
        if items > 0:
            eng.wait_ge(self.semh, self.base + self.step * items)


class ChN:

    def __init__(self, b, name, nslots=2, per_item=1):
        self.b = b
        self.n = nslots
        self.per = per_item
        self.names = [f"{name}_{i}" for i in range(nslots)]
        self.sems = [b.sem(nm) for nm in self.names]
        self.bases = [b.cnt[nm] for nm in self.names]

    def inc(self, inst, k):
        nm = self.names[k % self.n]
        inst.then_inc(self.sems[k % self.n], 16)
        self.b.cnt[nm] += 16
        return inst

    def wait(self, eng, k):
        s = k % self.n
        eng.wait_ge(self.sems[s], self.bases[s] + 16 * self.per * (k // self.n + 1))

    def wait_all(self, eng):
        for s in range(self.n):
            if self.b.cnt[self.names[s]] > self.bases[s]:
                eng.wait_ge(self.sems[s], self.b.cnt[self.names[s]])


def phase(b, emitters, name=""):
    b.phase_idx += 1
    target = 5 * (b.phase_idx - 1)
    ph = b.sem("ph")
    for ename in ENGINES:
        fn = emitters.get(ename)

        def body(eng, fn=fn, ename=ename):
            if target > 0:
                eng.wait_ge(ph, target)
            if fn is not None:
                fn(eng)
            if ename == "sync":
                eng.sem_inc(ph, 1)
            else:
                if fn is not None:
                    eng.drain()
                eng.nop().then_inc(ph, 1)

        getattr(b.block, ename)(body)
    b.cnt["ph"] += 5



def build_program():
    with ExitStack() as ctx:
        b = Bld(ctx)
        nc = b.nc

        def din(name, shape, dt):
            return nc.dram_tensor(name, list(shape), dt, kind="ExternalInput").ap()

        x_in = din("x_in", [TPC, D_], F32)
        wqkv_sl = din("wqkv_sl", [D_, 384], F32)
        wout_sl = din("wout_sl", [256, D_], F32)
        wq_sl = din("wq_sl", [D_, 256], F32)
        wk_sl = din("wk_sl", [D_, 256], F32)
        wv_sl = din("wv_sl", [D_, 256], F32)
        wo_sl = din("wo_sl", [256, D_], F32)
        bq_in = din("bq_sl", [2, 128], F32)
        bk_in = din("bk_sl", [2, 128], F32)
        bv_in = din("bv_sl", [2, 128], F32)
        bo_in = din("bo", [16, 128], F32)
        gpk = din("gpk", [NSL, 1024], U8)
        gsc = din("gsc", [NSL, 64], F32)
        upk = din("upk", [NSL, 1024], U8)
        usc = din("usc", [NSL, 64], F32)
        dpk = din("dpk", [D_, 1024], U8)
        dsc = din("dsc", [D_, 64], F32)
        cos_in = din("cos_t", [128, 8192], F32)
        sin_in = din("sin_t", [128, 8192], F32)
        toep_in = din("toep", [128, 1664], F32)
        ident_in = din("ident", [128, 128], F32)
        sink_in = din("sink", [1, 4], F32)
        y_out = nc.dram_tensor("y", [TPC, D_], F32, kind="ExternalOutput").ap()

        def dint(name, shape, dt):
            return nc.dram_tensor(name, list(shape), dt, kind="Internal").ap()

        wgT = dint("wgT", [D_, NSL], BF16)
        wuT = dint("wuT", [D_, NSL], BF16)
        wdT = dint("wdT", [NSL, D_], BF16)
        lnx_src = dint("lnx_src", [D_, TPC], BF16)
        gx = dint("gx", [NCORE, D_, TPC], BF16)
        qT_h = dint("qT_h", [256, 8192], BF16)
        vT_h = dint("vT_h", [256, 8192], BF16)
        psrc = dint("psrc", [NCORE, D_, TPC], BF16)
        pdst = dint("pdst", [D_, TPC], BF16)
        psrc_a = dint("psrc_a", [NCORE, D_, TPC], BF16)
        pdst_a = dint("pdst_a", [D_, TPC], BF16)

        rg = [list(range(NCORE))]

        xT = ctx.enter_context(nc.sbuf_tensor("xT", [128, NCH, TPC], F32))
        identf = ctx.enter_context(nc.sbuf_tensor("identf", [128, 128], F32))
        identb = ctx.enter_context(nc.sbuf_tensor("identb", [128, 128], BF16))
        ones_r = ctx.enter_context(nc.sbuf_tensor("ones_r", [1, 512], F32))
        ones_cf = ctx.enter_context(nc.sbuf_tensor("ones_cf", [128, 1], F32))
        ones_cb = ctx.enter_context(nc.sbuf_tensor("ones_cb", [128, 1], BF16))
        colm20 = ctx.enter_context(nc.sbuf_tensor("colm20", [128, 1], F32))
        sinkraw = ctx.enter_context(nc.sbuf_tensor("sinkraw", [1, 4], F32))
        sinkx = ctx.enter_context(nc.sbuf_tensor("sinkx", [1, 4], F32))

        b.block = ctx.enter_context(nc.Block())

        def p0a():
            lctx = ExitStack()
            ch_ldc = ChN(b, "p0_ldc", 1)
            ch_ldx = ChN(b, "p0_ldx", 2)
            ch_mm = Ch(b, "s_mm")
            ch_ev = Ch(b, "s_ev")
            xbuf = lctx.enter_context(nc.sbuf_tensor("p0_xbuf", [128, 2, D_], F32))
            pT = lctx.enter_context(nc.psum_tensor("p0_pT", [128, 2, 512], F32))

            def e_sync(eng):
                ch_ldc.inc(eng.dma_start(out=identf[:], in_=ident_in[:]), 0)
                for t in range(8):
                    if t >= 2:
                        ch_mm.wait(eng, 16 * (t - 1))
                    ch_ldx.inc(eng.dma_start(out=xbuf[:, t % 2, :],
                                             in_=x_in[t * 128:(t + 1) * 128, :]), t)
                ch_ldc.wait_all(eng)
                ch_ldx.wait_all(eng)

            def e_gpsimd(eng):
                eng.memset(ones_r[:], 1.0)
                eng.memset(ones_cf[:], 1.0)
                eng.memset(ones_cb[:], 1.0)
                eng.memset(colm20[:], EXPS)

            def e_tensor(eng):
                ch_ldc.wait(eng, 0)
                for t in range(8):
                    ch_ldx.wait(eng, t)
                    for f in range(16):
                        k = t * 16 + f
                        if k >= 2:
                            ch_ev.wait(eng, k - 1)
                        ch_mm.inc(eng.matmul(pT[:, k % 2, 0:128],
                                             xbuf[:, t % 2, f * 128:(f + 1) * 128],
                                             identf[:], start=True, stop=True,
                                             is_transpose=True))

            def e_vector(eng):
                ch_ldc.wait(eng, 0)
                eng.tensor_copy(identb[:], identf[:])
                for t in range(8):
                    for f in range(16):
                        k = t * 16 + f
                        ch_mm.wait(eng, k + 1)
                        ch_ev.inc(eng.tensor_copy(xT[:, f, t * 128:(t + 1) * 128],
                                                  pT[:, k % 2, 0:128]))

            phase(b, {"sync": e_sync, "gpsimd": e_gpsimd, "tensor": e_tensor,
                      "vector": e_vector}, "p0a")
            lctx.close()

        p0a()

        def p0b():
            lctx = ExitStack()
            ch_ld = ChN(b, "dq_ld", 2, per_item=2)
            ch_dq = Ch(b, "s_ve")
            ch_mm = Ch(b, "s_mm")
            ch_ev = Ch(b, "s_ev2")
            ch_st = ChN(b, "dq_st", 2)
            pk = lctx.enter_context(nc.sbuf_tensor("dq_pk", [128, 2, 1024], U8))
            sc = lctx.enter_context(nc.sbuf_tensor("dq_sc", [128, 2, 64], F32))
            u1 = lctx.enter_context(nc.sbuf_tensor("dq_u1", [128, 1024], U8))
            u2 = lctx.enter_context(nc.sbuf_tensor("dq_u2", [128, 1024], U8))
            u3 = lctx.enter_context(nc.sbuf_tensor("dq_u3", [128, 1024], U8))
            u4 = lctx.enter_context(nc.sbuf_tensor("dq_u4", [128, 1024], U8))
            f1 = lctx.enter_context(nc.sbuf_tensor("dq_f1", [128, 1024], F32))
            f2 = lctx.enter_context(nc.sbuf_tensor("dq_f2", [128, 1024], F32))
            f3 = lctx.enter_context(nc.sbuf_tensor("dq_f3", [128, 1024], F32))
            f4 = lctx.enter_context(nc.sbuf_tensor("dq_f4", [128, 1024], F32))
            wnat = lctx.enter_context(nc.sbuf_tensor("dq_wnat", [128, 16, D_], BF16))
            stg = lctx.enter_context(nc.sbuf_tensor("dq_stg", [128, 2, D_], BF16))
            pT = lctx.enter_context(nc.psum_tensor("dq_pT", [128, 4, 1024], BF16))

            mats = [(gpk, gsc, wgT), (upk, usc, wuT), (dpk, dsc, wdT)]

            def e_sync(eng):
                for mi, (pkin, scin, _) in enumerate(mats):
                    for i in range(16):
                        k = mi * 16 + i
                        if k >= 2:
                            ch_dq.wait(eng, k - 1)
                        ch_ld.inc(eng.dma_start(out=pk[:, k % 2, :],
                                                in_=pkin[i * 128:(i + 1) * 128, :]), k)
                        ch_ld.inc(eng.dma_start(out=sc[:, k % 2, :],
                                                in_=scin[i * 128:(i + 1) * 128, :]), k)
                ch_ld.wait_all(eng)

            def deq_tile(eng, k):
                src = pk[:, k % 2, :]
                eng.tensor_single_scalar(u1[:], src, 15, ALU.bitwise_and)
                eng.tensor_single_scalar(u2[:], src, 4, ALU.logical_shift_right)
                eng.drain()
                for phs, u in ((0, u1), (1, u2)):
                    eng.tensor_single_scalar(u3[:], u[:], 7, ALU.bitwise_and)
                    eng.tensor_single_scalar(u4[:], u[:], 3, ALU.logical_shift_right)
                    eng.drain()
                    eng.tensor_copy(f1[:], u3[:])
                    eng.tensor_copy(f2[:], u4[:])
                    eng.drain()
                    eng.tensor_scalar(f3[:], f1[:], 0.5, None, ALU.mult)
                    eng.tensor_scalar(f4[:], f1[:], -2.0, None, ALU.add)
                    eng.drain()
                    eng.tensor_tensor(f3[:], f3[:], f4[:], ALU.max)
                    eng.drain()
                    eng.tensor_scalar(f4[:], f1[:], 2.0, -8.0, ALU.mult, ALU.add)
                    eng.tensor_scalar(f2[:], f2[:], -2.0, 1.0, ALU.mult, ALU.add)
                    eng.drain()
                    eng.tensor_tensor(f3[:], f3[:], f4[:], ALU.max)
                    eng.drain()
                    eng.tensor_tensor(f3[:], f3[:], f2[:], ALU.mult)
                    eng.drain()
                    scap = sc[:, k % 2, :].rearrange("p (bb o) -> p bb o", o=1)
                    scb = bass.AP(scap.tensor, scap.offset,
                                  [list(xx) for xx in scap.ap[:-1]] + [[0, 16]])
                    lhs = f3[:].rearrange("p (bb o) -> p bb o", o=16)
                    outv = wnat[:, k % 16, :].rearrange("p (cc two) -> p cc two", two=2)[:, :, phs]
                    outv = outv.rearrange("p (bb o) -> p bb o", o=16)
                    eng.tensor_tensor(outv, lhs, scb, ALU.mult)
                    eng.drain()

            def e_vector(eng):
                for mi in range(3):
                    for i in range(16):
                        k = mi * 16 + i
                        ch_ld.wait(eng, k)
                        if mi > 0 and i == 0:
                            ch_mm.wait(eng, mi * 256)
                        deq_tile(eng, k)
                        ch_dq.inc(eng.engine_nop())

            def e_tensor(eng):
                for mi in range(3):
                    ch_dq.wait(eng, mi * 16 + 16)
                    for j in range(16):
                        for i in range(16):
                            k = mi * 256 + j * 16 + i
                            if k >= 4:
                                ch_ev.wait(eng, k - 3)
                            ch_mm.inc(eng.matmul(pT[:, k % 4, 0:128],
                                                 wnat[:, i, j * 128:(j + 1) * 128],
                                                 identb[:], start=True, stop=True,
                                                 is_transpose=True))

            def e_scalar(eng):
                for mi, (_, _, wdst) in enumerate(mats):
                    for j in range(16):
                        jg = mi * 16 + j
                        if jg >= 2:
                            ch_st.wait(eng, jg - 2)
                        for i in range(16):
                            k = mi * 256 + j * 16 + i
                            ch_mm.wait(eng, k + 1)
                            ch_ev.inc(eng.copy(stg[:, jg % 2, i * 128:(i + 1) * 128],
                                               pT[:, k % 4, 0:128]))
                        eng.drain()
                        ch_st.inc(eng.dma_start(out=wdst[j * 128:(j + 1) * 128, :],
                                                in_=stg[:, jg % 2, :]), jg)
                ch_st.wait_all(eng)

            def e_gpsimd(eng):
                gld = ChN(b, "dq_gld", 1)
                gld.inc(eng.dma_start(out=sinkraw[:], in_=sink_in[:]), 0)
                eng.wait_ge(gld.sems[0], gld.bases[0] + 16)

            phase(b, {"sync": e_sync, "vector": e_vector, "tensor": e_tensor,
                      "scalar": e_scalar, "gpsimd": e_gpsimd}, "p0b")
            lctx.close()

        if "p0b" not in SKIP:
            p0b()

        if DEBUG_DEQ:
            dmp = [(wgT, nc.dram_tensor("dbg_wgT", [D_, NSL], BF16, kind="ExternalOutput").ap()),
                   (wuT, nc.dram_tensor("dbg_wuT", [D_, NSL], BF16, kind="ExternalOutput").ap()),
                   (wdT, nc.dram_tensor("dbg_wdT", [NSL, D_], BF16, kind="ExternalOutput").ap())]

            def pdump():
                ch = ChN(b, "dbg_st", 1, per_item=3)

                def e_sync(eng):
                    for src, dst in dmp:
                        ch.inc(eng.dma_start(out=dst[:], in_=src[:]), 0)
                    ch.wait_all(eng)

                phase(b, {"sync": e_sync}, "pdump")

            pdump()

        def emit_ln(tag):
            lctx = ExitStack()
            ch_sq = Ch(b, "s_sc")
            ch_mm = Ch(b, "s_mm")
            ch_ve = Ch(b, "s_ve")
            ch_st = ChN(b, "ln_st", 2)
            ch_cc = Ch(b, "s_cc")
            sq = lctx.enter_context(nc.sbuf_tensor(f"ln_sq_{tag}", [128, 2, TPC], F32))
            rows = lctx.enter_context(nc.sbuf_tensor(f"ln_rows_{tag}", [1, 8, TPC], F32))
            tmp = lctx.enter_context(nc.sbuf_tensor(f"ln_tmp_{tag}", [128, TPC], F32))
            lnb = lctx.enter_context(nc.sbuf_tensor(f"ln_lnb_{tag}", [128, 2, TPC], BF16))
            ps_s = lctx.enter_context(nc.psum_tensor(f"ln_ps_s_{tag}", [1, TPC], F32))
            ps_q = lctx.enter_context(nc.psum_tensor(f"ln_ps_q_{tag}", [1, TPC], F32))
            ps_A = lctx.enter_context(nc.psum_tensor(f"ln_ps_A_{tag}", [128, TPC], F32))
            ps_B = lctx.enter_context(nc.psum_tensor(f"ln_ps_B_{tag}", [128, TPC], F32))

            def e_scalar(eng):
                for f in range(16):
                    if f >= 2:
                        ch_mm.wait(eng, 32 + (f - 1) * 2)
                    ch_sq.inc(eng.activation(sq[:, f % 2, :], xT[:, f, :], AF.Square))
                ch_ve.wait(eng, 1)
                ch_sq.inc(eng.activation(rows[0:1, 5, :], rows[0:1, 2, :], AF.Sqrt))
                for f in range(16):
                    ch_ve.wait(eng, 2 + (f + 1))
                    ch_st.inc(eng.dma_start(out=lnx_src[f * 128:(f + 1) * 128, :],
                                            in_=lnb[:, f % 2, :]), f)
                ch_st.wait_all(eng)

            def e_tensor(eng):
                for tau in range(2):
                    for f in range(16):
                        ch_mm.inc(eng.matmul(ps_s[0:1, tau * 512:(tau + 1) * 512], ones_cf[:],
                                             xT[:, f, tau * 512:(tau + 1) * 512],
                                             start=(f == 0), stop=(f == 15)))
                for f in range(16):
                    ch_sq.wait(eng, f + 1)
                    for tau in range(2):
                        ch_mm.inc(eng.matmul(ps_q[0:1, tau * 512:(tau + 1) * 512], ones_cf[:],
                                             sq[:, f % 2, tau * 512:(tau + 1) * 512],
                                             start=(f == 0), stop=(f == 15),
                                             skip_group_check=True))
                ch_ve.wait(eng, 2)
                for tau in range(2):
                    ch_mm.inc(eng.matmul(ps_A[:, tau * 512:(tau + 1) * 512], ones_r[0:1, 0:128],
                                         rows[0:1, 3, tau * 512:(tau + 1) * 512],
                                         start=True, stop=True))
                    ch_mm.inc(eng.matmul(ps_B[:, tau * 512:(tau + 1) * 512], ones_r[0:1, 0:128],
                                         rows[0:1, 4, tau * 512:(tau + 1) * 512],
                                         start=True, stop=True))

            def e_vector(eng):
                ch_mm.wait(eng, 64)
                inv = 1.0 / float(D_)
                eng.tensor_scalar(rows[0:1, 0, :], ps_s[0:1, :], inv, None, ALU.mult)
                eng.tensor_scalar(rows[0:1, 1, :], ps_q[0:1, :], inv, None, ALU.mult)
                eng.drain()
                eng.tensor_mul(rows[0:1, 6, :], rows[0:1, 0, :], rows[0:1, 0, :])
                eng.drain()
                eng.tensor_sub(rows[0:1, 2, :], rows[0:1, 1, :], rows[0:1, 6, :])
                eng.drain()
                ch_ve.inc(eng.tensor_scalar(rows[0:1, 2, :], rows[0:1, 2, :], EPS, None, ALU.add))
                ch_sq.wait(eng, 17)
                eng.reciprocal(rows[0:1, 3, :], rows[0:1, 5, :])
                eng.drain()
                eng.tensor_mul(rows[0:1, 4, :], rows[0:1, 0, :], rows[0:1, 3, :])
                eng.drain()
                ch_ve.inc(eng.tensor_scalar(rows[0:1, 4, :], rows[0:1, 4, :], -1.0, None, ALU.mult))
                ch_mm.wait(eng, 68)
                tslot = 0
                for f in range(16):
                    if f >= 2:
                        ch_st.wait(eng, f - 2)
                    eng.tensor_mul(tmp[:], xT[:, f, :], ps_A[:])
                    eng.drain()
                    ch_ve.inc(eng.tensor_add(lnb[:, f % 2, :], tmp[:], ps_B[:]))
                    eng.drain()

            def e_gpsimd(eng):
                ch_st.wait_all(eng)
                ch_cc.inc(eng.collective_compute("AllGather", ALU.bypass, replica_groups=rg,
                                                 ins=[lnx_src[:]], outs=[gx[:]]))

            phase(b, {"scalar": e_scalar, "tensor": e_tensor, "vector": e_vector,
                      "gpsimd": e_gpsimd}, f"ln_{tag}")
            lctx.close()

        def emit_ffn(tag):
            lctx = ExitStack()
            cc_at = b.cnt.get("s_cc", 0)
            ch_gx = ChN(b, "f_gx", 1)
            ch_gx0 = ChN(b, "f_gx0", 4)
            ch_wg = ChN(b, "f_wg", 2)
            ch_wu = ChN(b, "f_wu", 2)
            ch_wd = ChN(b, "f_wd", 2)
            ch_mmg = Ch(b, "f_mmg")
            ch_gl = Ch(b, "f_gl")
            ch_int = Ch(b, "f_int")
            ch_mmd = Ch(b, "f_mmd")
            ch_dev = Ch(b, "f_dev")
            ch_pw = ChN(b, "f_pw", 2)
            ch_cc = Ch(b, "s_cc")
            ch_gd = ChN(b, "f_gd", 2)
            ch_va = Ch(b, "f_va")
            gxbuf = lctx.enter_context(nc.sbuf_tensor(f"ff_gx_{tag}", [128, NCH, TPC], BF16))
            inter = lctx.enter_context(nc.sbuf_tensor(f"ff_int_{tag}", [128, NCH, TPC], BF16))
            wg_t = lctx.enter_context(nc.sbuf_tensor(f"ff_wg_{tag}", [128, 2, NCH, 128], BF16))
            wu_t = lctx.enter_context(nc.sbuf_tensor(f"ff_wu_{tag}", [128, 2, NCH, 128], BF16))
            wd_t = lctx.enter_context(nc.sbuf_tensor(f"ff_wd_{tag}", [128, 2, NCH, 128], BF16))
            glbuf = lctx.enter_context(nc.sbuf_tensor(f"ff_gl_{tag}", [128, 2, TPC], F32))
            pstg = lctx.enter_context(nc.sbuf_tensor(f"ff_ps_{tag}", [128, 2, TPC], BF16))
            dbuf = lctx.enter_context(nc.sbuf_tensor(f"ff_db_{tag}", [128, 2, TPC], F32))
            psA_g = lctx.enter_context(nc.psum_tensor(f"ff_pAg_{tag}", [128, TPC], F32))
            psA_u = lctx.enter_context(nc.psum_tensor(f"ff_pAu_{tag}", [128, TPC], F32))
            psB_g = lctx.enter_context(nc.psum_tensor(f"ff_pBg_{tag}", [128, TPC], F32))
            psB_u = lctx.enter_context(nc.psum_tensor(f"ff_pBu_{tag}", [128, TPC], F32))
            ps_g = (psA_g, psB_g)
            ps_u = (psA_u, psB_u)

            def wg_load(eng, k):
                i = k % 16
                ch_wg.inc(eng.dma_start(out=wg_t[:, k % 2, :, :],
                                        in_=wgT[:, i * 128:(i + 1) * 128]
                                        .rearrange("(c p) n -> p c n", p=128)), k)

            def e_sync(eng):
                wg_load(eng, 0)
                wg_load(eng, 1)
                b.cc_wait(eng, cc_at)
                for q in range(2):
                    ch_gx0.inc(eng.dma_start(
                        out=gxbuf[:, q * 4:(q + 1) * 4, :],
                        in_=gx[0].rearrange("(c p) t -> p c t", p=128)
                        [:, q * 4:(q + 1) * 4, :]), q)
                for s in range(8):
                    for i in range(16):
                        k = s * 16 + i
                        if k < 2:
                            continue
                        ch_mmg.wait(eng, k - 1)
                        wg_load(eng, k)
                    if s < 7:
                        ch_mmg.wait(eng, 16 * (s + 1))
                        ch_gx.inc(eng.dma_start(out=gxbuf[:],
                                                in_=gx[s + 1]
                                                .rearrange("(c p) t -> p c t", p=128)), s)
                ch_wg.wait_all(eng)

            def e_tensor(eng):
                for s in range(8):
                    if s == 0:
                        for q in range(4):
                            ch_gx0.wait(eng, q)
                    else:
                        ch_gx.wait(eng, s - 1)
                        ch_dev.wait(eng, 16 * s)
                    for i in range(16):
                        k = s * 16 + i
                        ch_wg.wait(eng, k)
                        ch_wu.wait(eng, k)
                        if k >= 2:
                            ch_int.wait(eng, k - 1)
                        for tau in range(2):
                            for hc in range(16):
                                mi_ = eng.matmul(ps_g[k % 2][:, tau * 512:(tau + 1) * 512],
                                                 wg_t[:, k % 2, hc, :],
                                                 gxbuf[:, hc, tau * 512:(tau + 1) * 512],
                                                 start=(hc == 0), stop=(hc == 15))
                            for hc in range(16):
                                mi_ = eng.matmul(ps_u[k % 2][:, tau * 512:(tau + 1) * 512],
                                                 wu_t[:, k % 2, hc, :],
                                                 gxbuf[:, hc, tau * 512:(tau + 1) * 512],
                                                 start=(hc == 0), stop=(hc == 15))
                        ch_mmg.inc(mi_)
                    ch_int.wait(eng, 16 * (s + 1))
                    for j in range(16):
                        k = s * 16 + j
                        ch_wd.wait(eng, k)
                        if k >= 2:
                            ch_dev.wait(eng, k - 1)
                        for tau in range(2):
                            for ic in range(16):
                                mi_ = eng.matmul(ps_g[k % 2][:, tau * 512:(tau + 1) * 512],
                                                 wd_t[:, k % 2, ic, :],
                                                 inter[:, ic, tau * 512:(tau + 1) * 512],
                                                 start=(ic == 0), stop=(ic == 15))
                        ch_mmd.inc(mi_)

            def e_scalar(eng):
                def wd_load(k):
                    j = k % 16
                    s_ = k // 16
                    if k >= 2:
                        ch_mmd.wait(eng, k - 1)
                    ch_wd.inc(eng.dma_start(out=wd_t[:, k % 2, :, :],
                                            in_=wdT[:, j * 128:(j + 1) * 128]
                                            .rearrange("(c p) n -> p c n", p=128)), k)

                wd_load(0)
                wd_load(1)
                for s in range(8):
                    for i in range(16):
                        k = s * 16 + i
                        ch_mmg.wait(eng, k + 1)
                        if k >= 2:
                            ch_int.wait(eng, k - 1)
                        ch_gl.inc(eng.activation(glbuf[:, k % 2, :], ps_g[k % 2][:],
                                                 AF.Gelu_apprx_tanh))
                        if i % 8 == 0 and s * 16 + i // 8 + 2 < 128:
                            pass
                    for j in range(16):
                        k = s * 16 + j
                        if k + 2 < 128:
                            wd_load(k + 2)
                        ch_mmd.wait(eng, k + 1)
                        if k >= 2:
                            ch_pw.wait(eng, k - 2)
                        ch_dev.inc(eng.copy(pstg[:, k % 2, :], ps_g[k % 2][:]))
                        eng.drain()
                        ch_pw.inc(eng.dma_start(out=psrc[s, j * 128:(j + 1) * 128, :],
                                                in_=pstg[:, k % 2, :]), k)
                ch_pw.wait_all(eng)
                ch_wd.wait_all(eng)

            def e_vector(eng):
                for s in range(8):
                    if s >= 1:
                        ch_mmd.wait(eng, 16 * s)
                    for i in range(16):
                        k = s * 16 + i
                        ch_gl.wait(eng, k + 1)
                        ch_int.inc(eng.tensor_mul(inter[:, i, :], glbuf[:, k % 2, :],
                                                  ps_u[k % 2][:]))
                for f in range(16):
                    ch_gd.wait(eng, f)
                    ch_va.inc(eng.tensor_add(xT[:, f, :], xT[:, f, :], dbuf[:, f % 2, :]))

            def wu_load(eng, k):
                i = k % 16
                ch_wu.inc(eng.dma_start(out=wu_t[:, k % 2, :, :],
                                        in_=wuT[:, i * 128:(i + 1) * 128]
                                        .rearrange("(c p) n -> p c n", p=128)), k)

            def e_gpsimd(eng):
                wu_load(eng, 0)
                wu_load(eng, 1)
                b.cc_wait(eng, cc_at)
                for q in range(2, 4):
                    ch_gx0.inc(eng.dma_start(
                        out=gxbuf[:, q * 4:(q + 1) * 4, :],
                        in_=gx[0].rearrange("(c p) t -> p c t", p=128)
                        [:, q * 4:(q + 1) * 4, :]), q)
                for k in range(2, 128):
                    ch_mmg.wait(eng, k - 1)
                    wu_load(eng, k)
                ch_wu.wait_all(eng)
                ch_pw.wait_all(eng)
                ch_cc.inc(eng.collective_compute("ReduceScatter", ALU.add, replica_groups=rg,
                                                 ins=[psrc[:]], outs=[pdst[:]]))
                b.cc_wait(eng)
                for f in range(16):
                    if f >= 2:
                        ch_va.wait(eng, f - 1)
                    ch_gd.inc(eng.dma_start(out=dbuf[:, f % 2, :],
                                            in_=pdst[f * 128:(f + 1) * 128, :]), f)
                ch_gd.wait_all(eng)

            phase(b, {"sync": e_sync, "tensor": e_tensor, "scalar": e_scalar,
                      "vector": e_vector, "gpsimd": e_gpsimd}, f"ffn_{tag}")
            lctx.close()

        def emit_attn(tag):
            A = (tag == "attn_a")
            octx = ExitStack()
            if A:
                kvT = octx.enter_context(nc.sbuf_tensor("at_kvT", [128, 8192], BF16))
                vh = octx.enter_context(nc.sbuf_tensor("at_vh", [128, 64, 65], BF16))
            else:
                kT = octx.enter_context(nc.sbuf_tensor("bt_kT", [128, 2, 8192], BF16))
                vh = octx.enter_context(nc.sbuf_tensor("bt_vh", [128, 2, 64, 128], BF16))
            mctx = ExitStack()

            def proj_phase():
                lctx = ExitStack()
                cc_at = b.cnt.get("s_cc", 0)
                nob = 3 if A else 6
                ch_gx = ChN(b, "a_gx", 2)
                ch_wl = ChN(b, "a_wl", 1, per_item=(3 if A else 6))
                ch_mmp = Ch(b, "a_mmp")
                ch_pe = Ch(b, "a_pe")
                ch_rp = Ch(b, "a_rp")
                ch_qs = ChN(b, "a_qs", 2, per_item=2)
                ch_vs = ChN(b, "a_vs", 2, per_item=2)
                ch_vt = ChN(b, "a_vt", 2)
                ch_vhm = Ch(b, "a_vhm")
                ch_vhe = Ch(b, "a_vhe")
                gxb = lctx.enter_context(nc.sbuf_tensor(
                    f"a_gxb_{tag}", [128, 2, NCH, 512] if A else [128, NCH, 512], BF16))
                qb = lctx.enter_context(nc.sbuf_tensor(f"a_qb_{tag}", [128, 2, 2, 512], BF16))
                xsh = (lctx.enter_context(nc.sbuf_tensor(f"a_xsh_{tag}", [128, 512], BF16))
                       if A else None)
                ps_p = lctx.enter_context(nc.psum_tensor(f"a_psp_{tag}", [128, 4, 512], F32))
                ps_vt = lctx.enter_context(nc.psum_tensor(f"a_pvt_{tag}", [128, 2, 1024], BF16))
                vtt = lctx.enter_context(nc.sbuf_tensor(f"a_vtt_{tag}", [128, 2, 128], BF16))
                if A:
                    wsl = lctx.enter_context(nc.sbuf_tensor("a_wsl", [128, NCH, 384], BF16))
                    cosb = lctx.enter_context(nc.sbuf_tensor("a_cos", [128, 8192], BF16))
                    sinb = lctx.enter_context(nc.sbuf_tensor("a_sin", [128, 8192], BF16))
                else:
                    wsl = lctx.enter_context(nc.sbuf_tensor("b_wsl", [128, NCH, 768], BF16))
                    vb = lctx.enter_context(nc.sbuf_tensor("b_vb", [128, 2, 2, 512], BF16))
                    bqs = lctx.enter_context(nc.sbuf_tensor("b_bqs", [128, 6], F32))

                def e_gpsimd(eng):
                    if A:
                        ch_wl.inc(eng.dma_start(
                            out=wsl[:], in_=wqkv_sl.rearrange("(c p) o -> p c o", p=128)), 0)
                        ch_wl.inc(eng.dma_start(out=cosb[:], in_=cos_in[:]), 0)
                        ch_wl.inc(eng.dma_start(out=sinb[:], in_=sin_in[:]), 0)
                    else:
                        for wi, src in enumerate((wq_sl, wk_sl, wv_sl)):
                            ch_wl.inc(eng.dma_start(
                                out=wsl[:, :, wi * 256:(wi + 1) * 256],
                                in_=src.rearrange("(c p) o -> p c o", p=128)), 0)
                        for wi, src in enumerate((bq_in, bk_in, bv_in)):
                            ch_wl.inc(eng.dma_start(
                                out=bqs[:, wi * 2:(wi + 1) * 2],
                                in_=src.rearrange("c p -> p c"),
                                allow_slow_non_contiguous=True), 0)
                    ch_wl.wait_all(eng)

                def e_scalar(eng):
                    if A:
                        ch_wl.wait(eng, 0)
                        eng.activation(sinkx[:], sinkraw[:], AF.Exp, bias=colm20[0:1, 0:1])
                    for tt in range(16):
                        if A:
                            if tt >= 2:
                                ch_qs.wait(eng, tt - 2)
                            for ob in range(nob):
                                g = tt * nob + ob
                                ch_mmp.wait(eng, g + 1)
                                slot = g % 4
                                if ob < 2:
                                    ch_pe.inc(eng.activation(qb[:, tt % 2, ob, :],
                                                             ps_p[:, slot, :], AF.Copy,
                                                             scale=SCALE_A))
                                else:
                                    ch_pe.inc(eng.copy(kvT[:, tt * 512:(tt + 1) * 512],
                                                       ps_p[:, slot, :]))
                            ch_rp.wait(eng, tt + 1)
                            eng.drain()
                        else:
                            ch_pe.wait(eng, (tt + 1) * nob)
                        for cchunk in range(2):
                            ch_qs.inc(eng.dma_start(
                                out=qT_h[cchunk * 128:(cchunk + 1) * 128,
                                         tt * 512:(tt + 1) * 512],
                                in_=qb[:, tt % 2, cchunk, :]), tt)
                        if not A:
                            for hh in range(2):
                                ch_vs.inc(eng.dma_start(
                                    out=vT_h[hh * 128:(hh + 1) * 128, tt * 512:(tt + 1) * 512],
                                    in_=vb[:, tt % 2, hh, :]), tt)
                    if A:
                        for kc in range(64):
                            ch_vhm.wait(eng, kc + 1)
                            ch_vhe.inc(eng.copy(vh[:, kc, 0:64],
                                                ps_vt[:, kc % 2, 0:64]))
                    else:
                        for g in range(128):
                            hh, kc = g // 64, g % 64
                            ch_vhm.wait(eng, g + 1)
                            ch_vhe.inc(eng.copy(vh[:, hh, kc, :],
                                                ps_vt[:, g % 2, 0:128]))
                    ch_qs.wait_all(eng)
                    ch_vs.wait_all(eng)

                def e_sync(eng):
                    b.cc_wait(eng, cc_at)
                    for tt in range(16):
                        if A:
                            if tt >= 2:
                                ch_mmp.wait(eng, (tt - 1) * nob)
                        elif tt >= 1:
                            ch_mmp.wait(eng, tt * nob)
                        s_, half = tt // 2, tt % 2
                        ch_gx.inc(eng.dma_start(
                            out=gxb[:, tt % 2] if A else gxb[:],
                            in_=gx[s_].rearrange("(c p) t -> p c t", p=128)
                            [:, :, half * 512:(half + 1) * 512]), tt)
                    if not A:
                        ch_vs.wait_all(eng)
                        for g in range(128):
                            hh, kc = g // 64, g % 64
                            if g >= 2:
                                ch_vhm.wait(eng, g - 1)
                            ch_vt.inc(eng.dma_start(
                                out=vtt[:, g % 2, :],
                                in_=vT_h[hh * 128:(hh + 1) * 128,
                                         kc * 128:(kc + 1) * 128]), g)

                def e_tensor(eng):
                    ch_wl.wait(eng, 0)
                    for tt in range(16):
                        ch_gx.wait(eng, tt)
                        for ob in range(nob):
                            g = tt * nob + ob
                            slot = g % 4
                            if g >= 4:
                                ch_pe.wait(eng, g - 3)
                            if A:
                                wcols = wsl[:, :, ob * 128:(ob + 1) * 128]
                            else:
                                wi, hh = ob // 2, ob % 2
                                wcols = wsl[:, :, wi * 256 + hh * 128: wi * 256 + (hh + 1) * 128]
                            gxs = gxb[:, tt % 2] if A else gxb[:]
                            for hc in range(16):
                                mi_ = eng.matmul(ps_p[:, slot, :], wcols[:, hc, :],
                                                 gxs[:, hc, :],
                                                 start=(hc == 0), stop=(hc == 15))
                            ch_mmp.inc(mi_)
                    ch_pe.wait(eng, 16 * nob)
                    if A:
                        for kc in range(64):
                            if kc >= 2:
                                ch_vhe.wait(eng, kc - 1)
                            ch_vhm.inc(eng.matmul(ps_vt[:, kc % 2, 0:64],
                                                  kvT[64:128, kc * 128:(kc + 1) * 128],
                                                  identb[64:128, 64:128], start=True,
                                                  stop=True, is_transpose=True))
                    else:
                        for g in range(128):
                            ch_vt.wait(eng, g)
                            if g >= 2:
                                ch_vhe.wait(eng, g - 1)
                            ch_vhm.inc(eng.matmul(ps_vt[:, g % 2, 0:128],
                                                  vtt[:, g % 2, :],
                                                  identb[:], start=True, stop=True,
                                                  is_transpose=True))

                def e_vector(eng):
                    if not A:
                        ch_wl.wait(eng, 0)
                        for tt in range(16):
                            if tt >= 2:
                                ch_qs.wait(eng, tt - 2)
                                ch_vs.wait(eng, tt - 2)
                            for ob in range(nob):
                                g = tt * nob + ob
                                ch_mmp.wait(eng, g + 1)
                                slot = g % 4
                                if ob < 2:
                                    ch_pe.inc(eng.tensor_scalar(
                                        qb[:, tt % 2, ob, :], ps_p[:, slot, :],
                                        bqs[:, ob:ob + 1], SCALE_B, ALU.add, ALU.mult))
                                elif ob < 4:
                                    hh = ob - 2
                                    ch_pe.inc(eng.tensor_scalar(
                                        kT[:, hh, tt * 512:(tt + 1) * 512],
                                        ps_p[:, slot, :], bqs[:, 2 + hh:3 + hh],
                                        None, ALU.add))
                                else:
                                    hh = ob - 4
                                    ch_pe.inc(eng.tensor_scalar(
                                        vb[:, tt % 2, hh, :], ps_p[:, slot, :],
                                        bqs[:, 4 + hh:5 + hh], None, ALU.add))
                        return
                    for tt in range(16):
                        base = tt * nob
                        ch_pe.wait(eng, base + 3)
                        if A:
                            if tt >= 2:
                                ch_qs.wait(eng, tt - 2)
                            cs = cosb[:, tt * 512:(tt + 1) * 512]
                            sn = sinb[:, tt * 512:(tt + 1) * 512]
                            for cchunk in range(2):
                                qap = qb[:, tt % 2, cchunk, :]
                                eng.drain()
                                for pb in (0, 64):
                                    eng.tensor_copy(xsh[pb:pb + 16, :], qap[pb + 32:pb + 48, :])
                                    eng.tensor_copy(xsh[pb + 32:pb + 48, :], qap[pb:pb + 16, :])
                                eng.drain()
                                for pb in (0, 32, 64, 96):
                                    eng.tensor_mul(qap[pb:pb + 16, :], qap[pb:pb + 16, :],
                                                   cs[pb:pb + 16, :])
                                    eng.tensor_mul(xsh[pb:pb + 16, :], xsh[pb:pb + 16, :],
                                                   sn[pb:pb + 16, :])
                                eng.drain()
                                for pb in (0, 32, 64, 96):
                                    eng.tensor_add(qap[pb:pb + 16, :], qap[pb:pb + 16, :],
                                                   xsh[pb:pb + 16, :])
                            kap = kvT[:, tt * 512:(tt + 1) * 512]
                            eng.drain()
                            eng.tensor_copy(xsh[0:16, :], kap[32:48, :])
                            eng.tensor_copy(xsh[32:48, :], kap[0:16, :])
                            eng.drain()
                            for pb in (0, 32):
                                eng.tensor_mul(kap[pb:pb + 16, :], kap[pb:pb + 16, :],
                                               cs[pb:pb + 16, :])
                                eng.tensor_mul(xsh[pb:pb + 16, :], xsh[pb:pb + 16, :],
                                               sn[pb:pb + 16, :])
                            eng.drain()
                            for pb in (0, 32):
                                eng.tensor_add(kap[pb:pb + 16, :], kap[pb:pb + 16, :],
                                               xsh[pb:pb + 16, :])
                        ch_rp.inc(eng.engine_nop())

                phase(b, {"gpsimd": e_gpsimd, "scalar": e_scalar, "sync": e_sync,
                          "tensor": e_tensor, "vector": e_vector}, f"{tag}_proj")
                lctx.close()

            def attn_phase():
                lctx = ExitStack()
                ch_qb = ChN(b, "a_qb", 2, per_item=(2 if A else 1))
                ch_tl = ChN(b, "a_tl", 1)
                ch_ms = Ch(b, "a_ms")
                ch_bm = Ch(b, "a_bm")
                ch_xp = Ch(b, "a_xp")
                ch_ma = Ch(b, "a_ma")
                ch_on = Ch(b, "a_on")
                ch_nf = Ch(b, "a_nf")
                ch_mn = Ch(b, "a_mn")
                ch_no = Ch(b, "a_no")
                qblk = lctx.enter_context(nc.sbuf_tensor(f"a_qblk_{tag}", [128, 2, 512], BF16))
                ptb = lctx.enter_context(nc.sbuf_tensor(f"a_pt_{tag}", [128, 4, 512], BF16))
                toepb = lctx.enter_context(nc.sbuf_tensor(f"a_toep_{tag}", [128, 1664], F32))
                tmpf = lctx.enter_context(nc.sbuf_tensor(f"a_tmpf_{tag}", [128, 2, 512], F32))
                d2r = lctx.enter_context(nc.sbuf_tensor(f"a_d2r_{tag}", [1, 2, 512], F32))
                nfr = lctx.enter_context(nc.sbuf_tensor(f"a_nfr_{tag}", [1, 2, 512], F32))
                drow = lctx.enter_context(nc.sbuf_tensor(f"a_dr_{tag}", [1, 512], F32))
                ps_s = lctx.enter_context(nc.psum_tensor(f"a_pss_{tag}", [128, 4, 512], F32))
                ps_oA = lctx.enter_context(nc.psum_tensor(f"a_poA_{tag}", [128, 512], F32))
                ps_oB = lctx.enter_context(nc.psum_tensor(f"a_poB_{tag}", [128, 512], F32))
                ps_n = lctx.enter_context(nc.psum_tensor(f"a_pn_{tag}", [128, 512], F32))
                ps_d = (lctx.enter_context(nc.psum_tensor(f"a_pd_{tag}", [1, 512], F32))
                        if not A else None)
                ps_o = (ps_oA, ps_oB)

                nvh = 64 if A else 128
                units = [(bb, m) for bb in range(2) for m in range(4)] if A \
                    else [(bb, hh) for bb in range(2) for hh in range(2)]
                iters = [(u, qb_) for u in units for qb_ in range(8)]
                nit = len(iters)
                band_counts = [band_js(qb_) if A else [] for (u, qb_) in iters]
                cum_band = []
                nb = 0
                for it in range(nit):
                    for j in range(32):
                        if j in band_counts[it]:
                            nb += 1
                        cum_band.append(nb)

                def e_sync(eng):
                    ch_tl.inc(eng.dma_start(out=toepb[:], in_=toep_in[:]), 0)
                    for it, (u, qb_) in enumerate(iters):
                        if it >= 2:
                            ch_ms.wait(eng, 32 * (it - 1))
                        rr = u[1] * (64 if A else 128)
                        bb = u[0]
                        src = qT_h[rr:rr + (64 if A else 128),
                                   bb * 4096 + qb_ * 512: bb * 4096 + (qb_ + 1) * 512]
                        ch_qb.inc(eng.dma_start(
                            out=qblk[0:(64 if A else 128), it % 2, :], in_=src), it)
                        if A:
                            ch_qb.inc(eng.dma_start(
                                out=qblk[64:128, it % 2, :], in_=src), it)

                on0 = 2 if A else 1

                def e_tensor(eng):
                    ch_on.wait(eng, on0)
                    for it, (u, qb_) in enumerate(iters):
                        bb, mh = u
                        ch_qb.wait(eng, it)
                        if it >= 2:
                            ch_on.wait(eng, on0 + 2 * (it - 1))
                        def av_pair(p):
                            ch_xp.wait(eng, it * 16 + p + 1)
                            for j in (2 * p, 2 * p + 1):
                                g = it * 32 + j
                                kcg = bb * 32 + j
                                av_l = vh[:, kcg, 0:65] if A else vh[:, mh, kcg, :]
                                mi_ = eng.matmul(ps_o[it % 2][0:(65 if A else 128), :], av_l,
                                                 ptb[:, g % 4, :],
                                                 start=(j == 0), stop=(j == 31))
                                if not A:
                                    mi_ = eng.matmul(ps_d[0:1, :], ones_cb[:],
                                                     ptb[:, g % 4, :],
                                                     start=(j == 0), stop=(j == 31),
                                                     skip_group_check=True)
                                ch_ma.inc(mi_)

                        for p in range(16):
                            if p == 6 and it >= 1:
                                itp = it - 1
                                ch_nf.wait(eng, itp + 1)
                                ch_no.wait(eng, itp)
                                ch_mn.inc(eng.matmul(ps_n[:], ones_r[0:1, 0:128],
                                                     nfr[0:1, itp % 2, :],
                                                     start=True, stop=True))
                            for j in (2 * p, 2 * p + 1):
                                g = it * 32 + j
                                if g >= 4:
                                    ch_xp.wait(eng, (g - 4) // 2 + 1)
                                kc0 = bb * 4096 + j * 128
                                if A:
                                    lo = 64 * (j & 1)
                                    lhs = kvT[lo:lo + 64, kc0:kc0 + 128]
                                    rhs = qblk[lo:lo + 64, it % 2, :]
                                else:
                                    lhs = kT[:, mh, kc0:kc0 + 128]
                                    rhs = qblk[:, it % 2, :]
                                ch_ms.inc(eng.matmul(ps_s[:, g % 4, :], lhs, rhs,
                                                     start=True, stop=True))
                            if p >= 2:
                                av_pair(p - 2)
                        av_pair(14)
                        av_pair(15)
                    itp = nit - 1
                    ch_nf.wait(eng, itp + 1)
                    ch_no.wait(eng, itp)
                    ch_mn.inc(eng.matmul(ps_n[:], ones_r[0:1, 0:128],
                                         nfr[0:1, itp % 2, :], start=True, stop=True))

                def e_scalar(eng):
                    for it, (u, qb_) in enumerate(iters):
                        for p in range(16):
                            g0 = it * 32 + 2 * p
                            ch_ms.wait(eng, g0 + 2)
                            if g0 >= 2:
                                ch_ma.wait(eng, g0 - 2)
                            if A and cum_band[g0 + 1] > 0:
                                ch_bm.wait(eng, cum_band[g0 + 1])
                            sl = (2 * p) % 4
                            ch_xp.inc(eng.activation(ptb[:, sl:sl + 2, :],
                                                     ps_s[:, sl:sl + 2, :],
                                                     AF.Exp, bias=colm20[:, 0:1]))
                        ch_ma.wait(eng, 32 * (it + 1))
                        if it >= 2:
                            ch_no.wait(eng, it - 1)
                        if A:
                            ch_on.inc(eng.copy(tmpf[0:64, it % 2, :], ps_o[it % 2][0:64, :]))
                            ch_on.inc(eng.copy(d2r[0:1, it % 2, :],
                                               ps_o[it % 2][64:65, :]))
                        else:
                            ch_on.inc(eng.copy(tmpf[:, it % 2, :], ps_o[it % 2][:]))
                            ch_on.inc(eng.copy(d2r[0:1, it % 2, :], ps_d[0:1, :]))

                def e_vector(eng):
                    if A:
                        ch_on.inc(eng.memset(vh[:, :, 64:65], 1.0))
                        eng.tensor_copy(kvT[64:128, 0:4096], kvT[0:64, 0:4096])
                        eng.drain()
                        ch_on.inc(eng.tensor_copy(kvT[64:128, 4096:8192],
                                                  kvT[0:64, 4096:8192]))
                        eng.drain()
                    else:
                        ch_on.inc(eng.engine_nop())
                    ch_tl.wait(eng, 0)

                    def epilogue(it):
                        u, qb_ = iters[it]
                        ch_on.wait(eng, on0 + 2 * (it + 1))
                        if A:
                            eng.tensor_single_scalar(drow[:], d2r[0:1, it % 2, :],
                                                     sinkx[0:1, u[1]:u[1] + 1], ALU.add)
                            eng.drain()
                            eng.reciprocal(drow[:], drow[:])
                        else:
                            eng.reciprocal(drow[:], d2r[0:1, it % 2, :])
                        eng.drain()
                        ch_nf.inc(eng.tensor_scalar(nfr[0:1, it % 2, :], drow[:],
                                                    1.0, None, ALU.mult))
                        ch_mn.wait(eng, it + 1)
                        bb = u[0]
                        col = bb * 4096 + qb_ * 512
                        if A:
                            mh = u[1]
                            dst = aout[(mh % 2) * 64:(mh % 2) * 64 + 64, mh // 2,
                                       col:col + 512]
                            ch_no.inc(eng.tensor_mul(dst, tmpf[0:64, it % 2, :],
                                                     ps_n[0:64, :]))
                        else:
                            hh = u[1]
                            dst = aout[:, hh, col:col + 512]
                            ch_no.inc(eng.tensor_mul(dst, tmpf[:, it % 2, :],
                                                     ps_n[:]))

                    def band_tt(it, qb_, j):
                        g = it * 32 + j
                        ch_ms.wait(eng, g + 1)
                        d0 = 512 * qb_ - 128 * j
                        ch_bm.inc(eng.tensor_add(ps_s[:, g % 4, :], ps_s[:, g % 4, :],
                                                 toepb[:, d0 + 511:d0 + 1023]))

                    for it, (u, qb_) in enumerate(iters):
                        bl = band_counts[it]
                        for j in bl:
                            if j < 8:
                                band_tt(it, qb_, j)
                        if it >= 1:
                            epilogue(it - 1)
                        for j in bl:
                            if j >= 8:
                                band_tt(it, qb_, j)
                    epilogue(nit - 1)

                def e_gpsimd(eng):
                    ch_no.wait(eng, nit)

                phase(b, {"sync": e_sync, "tensor": e_tensor, "scalar": e_scalar,
                          "vector": e_vector, "gpsimd": e_gpsimd}, f"{tag}_attn")
                lctx.close()

            def aproj_phase():
                lctx = ExitStack()
                ch_wl = ChN(b, "ap_wl", 1)
                ch_mw = Ch(b, "ap_mw")
                ch_ev = Ch(b, "ap_ev")
                ch_pw = ChN(b, "ap_pw", 2)
                ch_cc = Ch(b, "s_cc")
                ch_gd = ChN(b, "ap_gd", 2)
                ch_va = Ch(b, "ap_va")
                ch_bo = ChN(b, "ap_bo", 1)
                wcb = lctx.enter_context(nc.sbuf_tensor(f"ap_wcb_{tag}", [128, 2, D_], BF16))
                pstg = lctx.enter_context(nc.sbuf_tensor(f"ap_ps_{tag}", [128, 2, 512], BF16))
                dbuf = lctx.enter_context(nc.sbuf_tensor(f"ap_db_{tag}", [128, 2, TPC], F32))
                bob = lctx.enter_context(nc.sbuf_tensor(f"ap_bo_{tag}", [128, 16], F32))
                ps_w = lctx.enter_context(nc.psum_tensor(f"ap_psw_{tag}", [128, 4, 512], F32))
                wsrc = wout_sl if A else wo_sl

                def e_gpsimd(eng):
                    ch_wl.inc(eng.dma_start(
                        out=wcb[:], in_=wsrc.rearrange("(c p) o -> p c o", p=128)), 0)
                    ch_wl.wait_all(eng)
                    ch_pw.wait_all(eng)
                    ch_cc.inc(eng.collective_compute("ReduceScatter", ALU.add,
                                                     replica_groups=rg,
                                                     ins=[psrc_a[:]], outs=[pdst_a[:]]))
                    b.cc_wait(eng)
                    for f in range(16):
                        if f >= 2:
                            ch_va.wait(eng, f - 1)
                        ch_gd.inc(eng.dma_start(out=dbuf[:, f % 2, :],
                                                in_=pdst_a[f * 128:(f + 1) * 128, :]), f)
                    ch_gd.wait_all(eng)

                def e_tensor(eng):
                    ch_wl.wait(eng, 0)
                    for g in range(256):
                        oc, tt = g // 16, g % 16
                        if g >= 4:
                            ch_ev.wait(eng, g - 3)
                        for fc in range(2):
                            mi_ = eng.matmul(ps_w[:, g % 4, :],
                                             wcb[:, fc, oc * 128:(oc + 1) * 128],
                                             aout[:, fc, tt * 512:(tt + 1) * 512],
                                             start=(fc == 0), stop=(fc == 1))
                        ch_mw.inc(mi_)

                def e_scalar(eng):
                    for g in range(256):
                        oc, tt = g // 16, g % 16
                        ch_mw.wait(eng, g + 1)
                        if g >= 2:
                            ch_pw.wait(eng, g - 2)
                        ch_ev.inc(eng.copy(pstg[:, g % 2, :], ps_w[:, g % 4, :]))
                        eng.drain()
                        ch_pw.inc(eng.dma_start(
                            out=psrc_a[tt // 2, oc * 128:(oc + 1) * 128,
                                       (tt % 2) * 512:(tt % 2) * 512 + 512],
                            in_=pstg[:, g % 2, :]), g)
                    ch_pw.wait_all(eng)

                def e_sync(eng):
                    if not A:
                        ch_bo.inc(eng.dma_start(
                            out=bob[:], in_=bo_in.rearrange("c p -> p c"),
                            allow_slow_non_contiguous=True), 0)
                        ch_bo.wait_all(eng)

                def e_vector(eng):
                    for f in range(16):
                        ch_gd.wait(eng, f)
                        ch_va.inc(eng.tensor_add(xT[:, f, :], xT[:, f, :],
                                                 dbuf[:, f % 2, :]))

                def e_scalar_post(eng):
                    pass

                emitters = {"gpsimd": e_gpsimd, "tensor": e_tensor, "scalar": e_scalar,
                            "sync": e_sync, "vector": e_vector}
                phase(b, emitters, f"{tag}_aproj")

                if not A:
                    def e_vec_b(eng):
                        for f in range(16):
                            eng.tensor_scalar(xT[:, f, :], xT[:, f, :],
                                              bob[:, f:f + 1], None, ALU.add)

                    phase(b, {"vector": e_vec_b}, f"{tag}_bias")
                lctx.close()

            proj_phase()
            aout = mctx.enter_context(nc.sbuf_tensor(f"a_ao_{tag}", [128, 2, 8192], BF16))
            attn_phase()
            if DEBUG_ATTN:
                dq_ = nc.dram_tensor(f"dbg_qT_{tag}", [256, 8192], BF16,
                                     kind="ExternalOutput").ap()
                dk_ = nc.dram_tensor(f"dbg_kv_{tag}", [128, 8192], BF16,
                                     kind="ExternalOutput").ap()
                da_ = nc.dram_tensor(f"dbg_ao_{tag}", [128, 2, 8192], BF16,
                                     kind="ExternalOutput").ap()
                dv_ = nc.dram_tensor(f"dbg_vh_{tag}", [128, 64, 65], BF16,
                                     kind="ExternalOutput").ap()

                def pdbg():
                    ch = ChN(b, "dbg_a", 1, per_item=4)

                    def e_sync(eng):
                        ch.inc(eng.dma_start(out=dq_[:], in_=qT_h[:]), 0)
                        if A:
                            ch.inc(eng.dma_start(out=dk_[:], in_=kvT[:]), 0)
                            ch.inc(eng.dma_start(out=dv_[:], in_=vh[:]), 0)
                        else:
                            ch.inc(eng.dma_start(out=dk_[:], in_=kT[:, 0, :]), 0)
                            ch.inc(eng.dma_start(out=dv_[:], in_=vh[:, 0, :, 0:65]), 0)
                        ch.inc(eng.dma_start(out=da_[:], in_=aout[:]), 0)
                        ch.wait_all(eng)

                    phase(b, {"sync": e_sync}, f"{tag}_dbg")

                pdbg()
            aproj_phase()
            mctx.close()
            octx.close()


        for sub in ORDER:
            if sub in ("ffn1", "ffn2"):
                emit_ln(sub)
                emit_ffn(sub)
            else:
                emit_ln(sub)
                emit_attn(sub)

        def p9():
            lctx = ExitStack()
            ch_mm = Ch(b, "s_mm")
            ch_ev = Ch(b, "s_ve")
            ch_st = ChN(b, "p9_st", 2)
            ybuf = lctx.enter_context(nc.sbuf_tensor("p9_ybuf", [128, 2, D_], F32))
            pT = lctx.enter_context(nc.psum_tensor("p9_pT", [128, 2, 512], F32))

            def e_tensor(eng):
                for t in range(8):
                    for f in range(16):
                        k = t * 16 + f
                        if k >= 2:
                            ch_ev.wait(eng, k - 1)
                        ch_mm.inc(eng.matmul(pT[:, k % 2, 0:128], xT[:, f, t * 128:(t + 1) * 128],
                                             identf[:], start=True, stop=True,
                                             is_transpose=True))

            def e_vector(eng):
                for t in range(8):
                    if t >= 2:
                        ch_st.wait(eng, t - 2)
                    for f in range(16):
                        k = t * 16 + f
                        ch_mm.wait(eng, k + 1)
                        ch_ev.inc(eng.tensor_copy(ybuf[:, t % 2, f * 128:(f + 1) * 128],
                                                  pT[:, k % 2, 0:128]))

            def e_sync(eng):
                for t in range(8):
                    ch_ev.wait(eng, 16 * (t + 1))
                    ch_st.inc(eng.dma_start(out=y_out[t * 128:(t + 1) * 128, :],
                                            in_=ybuf[:, t % 2, :]), t)
                ch_st.wait_all(eng)

            phase(b, {"tensor": e_tensor, "vector": e_vector, "sync": e_sync}, "p9")
            lctx.close()

        p9()

        return nc


def kernel(**inputs):
    in_maps = stage_inputs(inputs)
    nc = build_program()
    res = run_bass_kernel_spmd(nc, in_maps, core_ids=list(range(NCORE)))
    out = np.empty((B_, L_, D_), np.float32)
    for c in range(NCORE):
        bb, r = c // 4, c % 4
        out[bb, r * TPC:(r + 1) * TPC, :] = np.asarray(res.results[c]["y"], np.float32)
    return out


# version 08 (final)
# speedup vs baseline: 1.0023x; 1.0023x over previous
import math
from contextlib import ExitStack

import numpy as np
import concourse.bass as bass
import concourse.mybir as mybir
from concourse.bass_utils import run_bass_kernel_spmd

F32 = mybir.dt.float32
BF16 = mybir.dt.bfloat16
U8 = mybir.dt.uint8
AF = mybir.ActivationFunctionType
ALU = mybir.AluOpType
AX = mybir.AxisListType

NCORE = 8
B_, L_, D_ = 2, 4096, 2048
TPC = 1024
NCH = 16
NSL = 2048
EXPS = -20.0
E20 = float(np.exp(np.float32(20.0)))
NEG = -1.0e30
EPS = 1e-5
SCALE_A = 0.125
SCALE_B = float(1.0 / np.sqrt(np.float32(128.0)))

DEBUG_ATTN = False
SKIP = set()
ORDER = ["attn_a", "ffn1", "attn_b", "ffn2"]
DEBUG_DEQ = False

ENGINES = ("tensor", "vector", "scalar", "gpsimd", "sync")



def _rope_tables():
    inv = ((1.0 / 1024.0) ** np.linspace(0.0, 1.0, 16)).astype(np.float32)
    pos = np.empty(8192, np.float32)
    for r in range(8):
        pos[r * 1024:(r + 1) * 1024] = (r % 4) * 1024 + np.arange(1024, dtype=np.float32)
    cos = np.zeros((128, 8192), np.float32)
    sin = np.zeros((128, 8192), np.float32)
    for r in range(128):
        m = r % 64
        if m < 16:
            d, sgn = m, 1.0
        elif 32 <= m < 48:
            d, sgn = m - 32, -1.0
        else:
            continue
        th = (pos.astype(np.float32) * inv[d]).astype(np.float32)
        cos[r] = np.cos(th).astype(np.float32)
        sin[r] = (sgn * np.sin(th)).astype(np.float32)
    return cos, sin


def _toeplitz():
    i = np.arange(128)[:, None]
    w = np.arange(1664)[None, :]
    d = (w - 511) - i
    return np.where((d >= 0) & (d < 512), np.float32(NEG), np.float32(0.0)).astype(np.float32)


def band_js(qb):
    return [j for j in range(32) if -511 <= 512 * qb - 128 * j <= 638]


def stage_inputs(inputs):
    x = np.asarray(inputs["L0_attn_a_x"], np.float32)
    wqkv = np.asarray(inputs["L0_attn_a_W_qkv"], np.float32)
    wout = np.ascontiguousarray(np.asarray(inputs["L0_attn_a_W_out"], np.float32))
    s_sink = np.asarray(inputs["L0_attn_a_s"], np.float32)
    wq = np.asarray(inputs["L0_attn_b_Wq"], np.float32)
    wk = np.asarray(inputs["L0_attn_b_Wk"], np.float32)
    wv = np.asarray(inputs["L0_attn_b_Wv"], np.float32)
    wo = np.asarray(inputs["L0_attn_b_Wo"], np.float32)
    bq = np.asarray(inputs["L0_attn_b_bq"], np.float32)
    bk = np.asarray(inputs["L0_attn_b_bk"], np.float32)
    bv = np.asarray(inputs["L0_attn_b_bv"], np.float32)
    bo = np.asarray(inputs["L0_attn_b_bo"], np.float32)
    gpk = np.asarray(inputs["L0_ffn_gate_W_packed"], np.int32)
    gsc = np.asarray(inputs["L0_ffn_gate_scales"], np.float32)
    upk = np.asarray(inputs["L0_ffn_up_W_packed"], np.int32)
    usc = np.asarray(inputs["L0_ffn_up_scales"], np.float32)
    dpk = np.asarray(inputs["L0_ffn_down_W_packed"], np.int32)
    dsc = np.asarray(inputs["L0_ffn_down_scales"], np.float32)

    cos, sin = _rope_tables()
    toep = _toeplitz()
    ident = np.eye(128, dtype=np.float32)

    def u8view(a):
        return np.ascontiguousarray(np.ascontiguousarray(a).view(np.uint8)[:, 0::4])

    wqT = np.ascontiguousarray(wq.T)
    wkT = np.ascontiguousarray(wk.T)
    wvT = np.ascontiguousarray(wv.T)
    woT = np.ascontiguousarray(wo.T)

    in_maps = []
    for c in range(NCORE):
        bb, r = c // 4, c % 4
        m = {}
        m["x_in"] = np.ascontiguousarray(x[bb, r * TPC:(r + 1) * TPC, :])
        qcols = wqkv[:, c * 256:(c + 1) * 256]
        kcols = wqkv[:, 2048 + c * 64:2048 + (c + 1) * 64]
        vcols = wqkv[:, 2560 + c * 64:2560 + (c + 1) * 64]
        m["wqkv_sl"] = np.ascontiguousarray(np.concatenate([qcols, kcols, vcols], axis=1))
        m["wout_sl"] = np.ascontiguousarray(wout[c * 256:(c + 1) * 256, :])
        m["wq_sl"] = np.ascontiguousarray(wqT[:, 2 * c * 128:(2 * c + 2) * 128])
        m["wk_sl"] = np.ascontiguousarray(wkT[:, 2 * c * 128:(2 * c + 2) * 128])
        m["wv_sl"] = np.ascontiguousarray(wvT[:, 2 * c * 128:(2 * c + 2) * 128])
        m["wo_sl"] = np.ascontiguousarray(woT[c * 256:(c + 1) * 256, :])
        m["bq_sl"] = np.ascontiguousarray(bq[2 * c * 128:(2 * c + 2) * 128].reshape(2, 128))
        m["bk_sl"] = np.ascontiguousarray(bk[2 * c * 128:(2 * c + 2) * 128].reshape(2, 128))
        m["bv_sl"] = np.ascontiguousarray(bv[2 * c * 128:(2 * c + 2) * 128].reshape(2, 128))
        m["bo"] = np.ascontiguousarray(bo.reshape(16, 128))
        m["gpk"] = u8view(gpk[c * NSL:(c + 1) * NSL, :])
        m["gsc"] = np.ascontiguousarray(gsc[c * NSL:(c + 1) * NSL, :])
        m["upk"] = u8view(upk[c * NSL:(c + 1) * NSL, :])
        m["usc"] = np.ascontiguousarray(usc[c * NSL:(c + 1) * NSL, :])
        m["dpk"] = u8view(dpk[:, c * 1024:(c + 1) * 1024])
        m["dsc"] = np.ascontiguousarray(dsc[:, c * 64:(c + 1) * 64])
        m["cos_t"] = cos
        m["sin_t"] = sin
        m["toep"] = toep
        m["ident"] = ident
        m["sink"] = np.ascontiguousarray(s_sink[0, c, :].reshape(1, 4))
        in_maps.append(m)
    return in_maps



class Bld:
    def __init__(self, ctx):
        self.nc = bass.Bass("TRN2", target_bir_lowering=False, debug=False)
        self.ctx = ctx
        self.sems = {}
        self.cnt = {}
        self.phase_idx = 0
        self.block = None

    def sem(self, name):
        if name not in self.sems:
            self.sems[name] = self.ctx.enter_context(self.nc.semaphore(name))
            self.cnt[name] = 0
        return self.sems[name]

    def cc_wait(self, eng, at=None):
        v = self.cnt.get("s_cc", 0) if at is None else at
        if v > 0:
            eng.wait_ge(self.sem("s_cc"), v)


class Ch:

    def __init__(self, b, name, step=1):
        self.b = b
        self.name = name
        self.step = step
        self.semh = b.sem(name)
        self.base = b.cnt[name]

    def inc(self, inst, items=1):
        inst.then_inc(self.semh, self.step * items)
        self.b.cnt[self.name] += self.step * items
        return inst

    def wait(self, eng, items):
        if items > 0:
            eng.wait_ge(self.semh, self.base + self.step * items)


class ChN:

    def __init__(self, b, name, nslots=2, per_item=1):
        self.b = b
        self.n = nslots
        self.per = per_item
        self.names = [f"{name}_{i}" for i in range(nslots)]
        self.sems = [b.sem(nm) for nm in self.names]
        self.bases = [b.cnt[nm] for nm in self.names]

    def inc(self, inst, k):
        nm = self.names[k % self.n]
        inst.then_inc(self.sems[k % self.n], 16)
        self.b.cnt[nm] += 16
        return inst

    def wait(self, eng, k):
        s = k % self.n
        eng.wait_ge(self.sems[s], self.bases[s] + 16 * self.per * (k // self.n + 1))

    def wait_all(self, eng):
        for s in range(self.n):
            if self.b.cnt[self.names[s]] > self.bases[s]:
                eng.wait_ge(self.sems[s], self.b.cnt[self.names[s]])


def phase(b, emitters, name=""):
    b.phase_idx += 1
    target = 5 * (b.phase_idx - 1)
    ph = b.sem("ph")
    for ename in ENGINES:
        fn = emitters.get(ename)

        def body(eng, fn=fn, ename=ename):
            if target > 0:
                eng.wait_ge(ph, target)
            if fn is not None:
                fn(eng)
            if ename == "sync":
                eng.sem_inc(ph, 1)
            else:
                if fn is not None:
                    eng.drain()
                eng.nop().then_inc(ph, 1)

        getattr(b.block, ename)(body)
    b.cnt["ph"] += 5



def build_program():
    with ExitStack() as ctx:
        b = Bld(ctx)
        nc = b.nc

        def din(name, shape, dt):
            return nc.dram_tensor(name, list(shape), dt, kind="ExternalInput").ap()

        x_in = din("x_in", [TPC, D_], F32)
        wqkv_sl = din("wqkv_sl", [D_, 384], F32)
        wout_sl = din("wout_sl", [256, D_], F32)
        wq_sl = din("wq_sl", [D_, 256], F32)
        wk_sl = din("wk_sl", [D_, 256], F32)
        wv_sl = din("wv_sl", [D_, 256], F32)
        wo_sl = din("wo_sl", [256, D_], F32)
        bq_in = din("bq_sl", [2, 128], F32)
        bk_in = din("bk_sl", [2, 128], F32)
        bv_in = din("bv_sl", [2, 128], F32)
        bo_in = din("bo", [16, 128], F32)
        gpk = din("gpk", [NSL, 1024], U8)
        gsc = din("gsc", [NSL, 64], F32)
        upk = din("upk", [NSL, 1024], U8)
        usc = din("usc", [NSL, 64], F32)
        dpk = din("dpk", [D_, 1024], U8)
        dsc = din("dsc", [D_, 64], F32)
        cos_in = din("cos_t", [128, 8192], F32)
        sin_in = din("sin_t", [128, 8192], F32)
        toep_in = din("toep", [128, 1664], F32)
        ident_in = din("ident", [128, 128], F32)
        sink_in = din("sink", [1, 4], F32)
        y_out = nc.dram_tensor("y", [TPC, D_], F32, kind="ExternalOutput").ap()

        def dint(name, shape, dt):
            return nc.dram_tensor(name, list(shape), dt, kind="Internal").ap()

        wgT = dint("wgT", [D_, NSL], BF16)
        wuT = dint("wuT", [D_, NSL], BF16)
        wdT = dint("wdT", [NSL, D_], BF16)
        lnx_src = dint("lnx_src", [D_, TPC], BF16)
        gx = dint("gx", [NCORE, D_, TPC], BF16)
        qT_h = dint("qT_h", [256, 8192], BF16)
        vT_h = dint("vT_h", [256, 8192], BF16)
        psrc = dint("psrc", [NCORE, D_, TPC], BF16)
        pdst = dint("pdst", [D_, TPC], BF16)
        psrc_a = dint("psrc_a", [NCORE, D_, TPC], BF16)
        pdst_a = dint("pdst_a", [D_, TPC], BF16)

        rg = [list(range(NCORE))]

        xT = ctx.enter_context(nc.sbuf_tensor("xT", [128, NCH, TPC], F32))
        identf = ctx.enter_context(nc.sbuf_tensor("identf", [128, 128], F32))
        identb = ctx.enter_context(nc.sbuf_tensor("identb", [128, 128], BF16))
        ones_r = ctx.enter_context(nc.sbuf_tensor("ones_r", [1, 512], F32))
        ones_cf = ctx.enter_context(nc.sbuf_tensor("ones_cf", [128, 1], F32))
        ones_cb = ctx.enter_context(nc.sbuf_tensor("ones_cb", [128, 1], BF16))
        colm20 = ctx.enter_context(nc.sbuf_tensor("colm20", [128, 1], F32))
        sinkraw = ctx.enter_context(nc.sbuf_tensor("sinkraw", [1, 4], F32))
        sinkx = ctx.enter_context(nc.sbuf_tensor("sinkx", [1, 4], F32))

        b.block = ctx.enter_context(nc.Block())

        def p0a():
            lctx = ExitStack()
            ch_ldc = ChN(b, "p0_ldc", 1)
            ch_ldx = ChN(b, "p0_ldx", 2)
            ch_mm = Ch(b, "s_mm")
            ch_ev = Ch(b, "s_ev")
            xbuf = lctx.enter_context(nc.sbuf_tensor("p0_xbuf", [128, 2, D_], F32))
            pT = lctx.enter_context(nc.psum_tensor("p0_pT", [128, 2, 512], F32))

            def e_sync(eng):
                ch_ldc.inc(eng.dma_start(out=identf[:], in_=ident_in[:]), 0)
                for t in range(8):
                    if t >= 2:
                        ch_mm.wait(eng, 16 * (t - 1))
                    ch_ldx.inc(eng.dma_start(out=xbuf[:, t % 2, :],
                                             in_=x_in[t * 128:(t + 1) * 128, :]), t)
                ch_ldc.wait_all(eng)
                ch_ldx.wait_all(eng)

            def e_gpsimd(eng):
                eng.memset(ones_r[:], 1.0)
                eng.memset(ones_cf[:], 1.0)
                eng.memset(ones_cb[:], 1.0)
                eng.memset(colm20[:], EXPS)

            def e_tensor(eng):
                ch_ldc.wait(eng, 0)
                for t in range(8):
                    ch_ldx.wait(eng, t)
                    for f in range(16):
                        k = t * 16 + f
                        if k >= 2:
                            ch_ev.wait(eng, k - 1)
                        ch_mm.inc(eng.matmul(pT[:, k % 2, 0:128],
                                             xbuf[:, t % 2, f * 128:(f + 1) * 128],
                                             identf[:], start=True, stop=True,
                                             is_transpose=True))

            def e_vector(eng):
                ch_ldc.wait(eng, 0)
                eng.tensor_copy(identb[:], identf[:])
                for t in range(8):
                    for f in range(16):
                        k = t * 16 + f
                        ch_mm.wait(eng, k + 1)
                        ch_ev.inc(eng.tensor_copy(xT[:, f, t * 128:(t + 1) * 128],
                                                  pT[:, k % 2, 0:128]))

            phase(b, {"sync": e_sync, "gpsimd": e_gpsimd, "tensor": e_tensor,
                      "vector": e_vector}, "p0a")
            lctx.close()

        p0a()

        def p0b():
            lctx = ExitStack()
            ch_ld = ChN(b, "dq_ld", 2, per_item=2)
            ch_dq = Ch(b, "s_ve")
            ch_mm = Ch(b, "s_mm")
            ch_ev = Ch(b, "s_ev2")
            ch_st = ChN(b, "dq_st", 2)
            pk = lctx.enter_context(nc.sbuf_tensor("dq_pk", [128, 2, 1024], U8))
            sc = lctx.enter_context(nc.sbuf_tensor("dq_sc", [128, 2, 64], F32))
            u1 = lctx.enter_context(nc.sbuf_tensor("dq_u1", [128, 1024], U8))
            u2 = lctx.enter_context(nc.sbuf_tensor("dq_u2", [128, 1024], U8))
            u3 = lctx.enter_context(nc.sbuf_tensor("dq_u3", [128, 1024], U8))
            u4 = lctx.enter_context(nc.sbuf_tensor("dq_u4", [128, 1024], U8))
            f1 = lctx.enter_context(nc.sbuf_tensor("dq_f1", [128, 1024], F32))
            f2 = lctx.enter_context(nc.sbuf_tensor("dq_f2", [128, 1024], F32))
            f3 = lctx.enter_context(nc.sbuf_tensor("dq_f3", [128, 1024], F32))
            f4 = lctx.enter_context(nc.sbuf_tensor("dq_f4", [128, 1024], F32))
            wnat = lctx.enter_context(nc.sbuf_tensor("dq_wnat", [128, 16, D_], BF16))
            stg = lctx.enter_context(nc.sbuf_tensor("dq_stg", [128, 2, D_], BF16))
            pT = lctx.enter_context(nc.psum_tensor("dq_pT", [128, 4, 1024], BF16))

            mats = [(gpk, gsc, wgT), (upk, usc, wuT), (dpk, dsc, wdT)]

            def e_sync(eng):
                for mi, (pkin, scin, _) in enumerate(mats):
                    for i in range(16):
                        k = mi * 16 + i
                        if k >= 2:
                            ch_dq.wait(eng, k - 1)
                        ch_ld.inc(eng.dma_start(out=pk[:, k % 2, :],
                                                in_=pkin[i * 128:(i + 1) * 128, :]), k)
                        ch_ld.inc(eng.dma_start(out=sc[:, k % 2, :],
                                                in_=scin[i * 128:(i + 1) * 128, :]), k)
                ch_ld.wait_all(eng)

            def deq_tile(eng, k):
                src = pk[:, k % 2, :]
                eng.tensor_single_scalar(u1[:], src, 15, ALU.bitwise_and)
                eng.tensor_single_scalar(u2[:], src, 4, ALU.logical_shift_right)
                eng.drain()
                for phs, u in ((0, u1), (1, u2)):
                    eng.tensor_single_scalar(u3[:], u[:], 7, ALU.bitwise_and)
                    eng.tensor_single_scalar(u4[:], u[:], 3, ALU.logical_shift_right)
                    eng.drain()
                    eng.tensor_copy(f1[:], u3[:])
                    eng.tensor_copy(f2[:], u4[:])
                    eng.drain()
                    eng.tensor_scalar(f3[:], f1[:], 0.5, None, ALU.mult)
                    eng.tensor_scalar(f4[:], f1[:], -2.0, None, ALU.add)
                    eng.drain()
                    eng.tensor_tensor(f3[:], f3[:], f4[:], ALU.max)
                    eng.drain()
                    eng.tensor_scalar(f4[:], f1[:], 2.0, -8.0, ALU.mult, ALU.add)
                    eng.tensor_scalar(f2[:], f2[:], -2.0, 1.0, ALU.mult, ALU.add)
                    eng.drain()
                    eng.tensor_tensor(f3[:], f3[:], f4[:], ALU.max)
                    eng.drain()
                    eng.tensor_tensor(f3[:], f3[:], f2[:], ALU.mult)
                    eng.drain()
                    scap = sc[:, k % 2, :].rearrange("p (bb o) -> p bb o", o=1)
                    scb = bass.AP(scap.tensor, scap.offset,
                                  [list(xx) for xx in scap.ap[:-1]] + [[0, 16]])
                    lhs = f3[:].rearrange("p (bb o) -> p bb o", o=16)
                    outv = wnat[:, k % 16, :].rearrange("p (cc two) -> p cc two", two=2)[:, :, phs]
                    outv = outv.rearrange("p (bb o) -> p bb o", o=16)
                    eng.tensor_tensor(outv, lhs, scb, ALU.mult)
                    eng.drain()

            def e_vector(eng):
                for mi in range(3):
                    for i in range(16):
                        k = mi * 16 + i
                        ch_ld.wait(eng, k)
                        if mi > 0 and i == 0:
                            ch_mm.wait(eng, mi * 256)
                        deq_tile(eng, k)
                        ch_dq.inc(eng.engine_nop())

            def e_tensor(eng):
                for mi in range(3):
                    ch_dq.wait(eng, mi * 16 + 16)
                    for j in range(16):
                        for i in range(16):
                            k = mi * 256 + j * 16 + i
                            if k >= 4:
                                ch_ev.wait(eng, k - 3)
                            ch_mm.inc(eng.matmul(pT[:, k % 4, 0:128],
                                                 wnat[:, i, j * 128:(j + 1) * 128],
                                                 identb[:], start=True, stop=True,
                                                 is_transpose=True))

            def e_scalar(eng):
                for mi, (_, _, wdst) in enumerate(mats):
                    for j in range(16):
                        jg = mi * 16 + j
                        if jg >= 2:
                            ch_st.wait(eng, jg - 2)
                        for i in range(16):
                            k = mi * 256 + j * 16 + i
                            ch_mm.wait(eng, k + 1)
                            ch_ev.inc(eng.copy(stg[:, jg % 2, i * 128:(i + 1) * 128],
                                               pT[:, k % 4, 0:128]))
                        eng.drain()
                        ch_st.inc(eng.dma_start(out=wdst[j * 128:(j + 1) * 128, :],
                                                in_=stg[:, jg % 2, :]), jg)
                ch_st.wait_all(eng)

            def e_gpsimd(eng):
                gld = ChN(b, "dq_gld", 1)
                gld.inc(eng.dma_start(out=sinkraw[:], in_=sink_in[:]), 0)
                eng.wait_ge(gld.sems[0], gld.bases[0] + 16)

            phase(b, {"sync": e_sync, "vector": e_vector, "tensor": e_tensor,
                      "scalar": e_scalar, "gpsimd": e_gpsimd}, "p0b")
            lctx.close()

        if "p0b" not in SKIP:
            p0b()

        if DEBUG_DEQ:
            dmp = [(wgT, nc.dram_tensor("dbg_wgT", [D_, NSL], BF16, kind="ExternalOutput").ap()),
                   (wuT, nc.dram_tensor("dbg_wuT", [D_, NSL], BF16, kind="ExternalOutput").ap()),
                   (wdT, nc.dram_tensor("dbg_wdT", [NSL, D_], BF16, kind="ExternalOutput").ap())]

            def pdump():
                ch = ChN(b, "dbg_st", 1, per_item=3)

                def e_sync(eng):
                    for src, dst in dmp:
                        ch.inc(eng.dma_start(out=dst[:], in_=src[:]), 0)
                    ch.wait_all(eng)

                phase(b, {"sync": e_sync}, "pdump")

            pdump()

        def emit_ln(tag):
            lctx = ExitStack()
            ch_sq = Ch(b, "s_sc")
            ch_mm = Ch(b, "s_mm")
            ch_ve = Ch(b, "s_ve")
            ch_st = ChN(b, "ln_st", 2)
            ch_cc = Ch(b, "s_cc")
            sq = lctx.enter_context(nc.sbuf_tensor(f"ln_sq_{tag}", [128, 2, TPC], F32))
            rows = lctx.enter_context(nc.sbuf_tensor(f"ln_rows_{tag}", [1, 8, TPC], F32))
            tmp = lctx.enter_context(nc.sbuf_tensor(f"ln_tmp_{tag}", [128, TPC], F32))
            lnb = lctx.enter_context(nc.sbuf_tensor(f"ln_lnb_{tag}", [128, 2, TPC], BF16))
            ps_s = lctx.enter_context(nc.psum_tensor(f"ln_ps_s_{tag}", [1, TPC], F32))
            ps_q = lctx.enter_context(nc.psum_tensor(f"ln_ps_q_{tag}", [1, TPC], F32))
            ps_A = lctx.enter_context(nc.psum_tensor(f"ln_ps_A_{tag}", [128, TPC], F32))
            ps_B = lctx.enter_context(nc.psum_tensor(f"ln_ps_B_{tag}", [128, TPC], F32))

            def e_scalar(eng):
                for f in range(16):
                    if f >= 2:
                        ch_mm.wait(eng, 32 + (f - 1) * 2)
                    ch_sq.inc(eng.activation(sq[:, f % 2, :], xT[:, f, :], AF.Square))
                ch_ve.wait(eng, 1)
                ch_sq.inc(eng.activation(rows[0:1, 5, :], rows[0:1, 2, :], AF.Sqrt))
                for f in range(16):
                    ch_ve.wait(eng, 2 + (f + 1))
                    ch_st.inc(eng.dma_start(out=lnx_src[f * 128:(f + 1) * 128, :],
                                            in_=lnb[:, f % 2, :]), f)
                ch_st.wait_all(eng)

            def e_tensor(eng):
                for tau in range(2):
                    for f in range(16):
                        ch_mm.inc(eng.matmul(ps_s[0:1, tau * 512:(tau + 1) * 512], ones_cf[:],
                                             xT[:, f, tau * 512:(tau + 1) * 512],
                                             start=(f == 0), stop=(f == 15)))
                for f in range(16):
                    ch_sq.wait(eng, f + 1)
                    for tau in range(2):
                        ch_mm.inc(eng.matmul(ps_q[0:1, tau * 512:(tau + 1) * 512], ones_cf[:],
                                             sq[:, f % 2, tau * 512:(tau + 1) * 512],
                                             start=(f == 0), stop=(f == 15),
                                             skip_group_check=True))
                ch_ve.wait(eng, 2)
                for tau in range(2):
                    ch_mm.inc(eng.matmul(ps_A[:, tau * 512:(tau + 1) * 512], ones_r[0:1, 0:128],
                                         rows[0:1, 3, tau * 512:(tau + 1) * 512],
                                         start=True, stop=True))
                    ch_mm.inc(eng.matmul(ps_B[:, tau * 512:(tau + 1) * 512], ones_r[0:1, 0:128],
                                         rows[0:1, 4, tau * 512:(tau + 1) * 512],
                                         start=True, stop=True))

            def e_vector(eng):
                ch_mm.wait(eng, 64)
                inv = 1.0 / float(D_)
                eng.tensor_scalar(rows[0:1, 0, :], ps_s[0:1, :], inv, None, ALU.mult)
                eng.tensor_scalar(rows[0:1, 1, :], ps_q[0:1, :], inv, None, ALU.mult)
                eng.drain()
                eng.tensor_mul(rows[0:1, 6, :], rows[0:1, 0, :], rows[0:1, 0, :])
                eng.drain()
                eng.tensor_sub(rows[0:1, 2, :], rows[0:1, 1, :], rows[0:1, 6, :])
                eng.drain()
                ch_ve.inc(eng.tensor_scalar(rows[0:1, 2, :], rows[0:1, 2, :], EPS, None, ALU.add))
                ch_sq.wait(eng, 17)
                eng.reciprocal(rows[0:1, 3, :], rows[0:1, 5, :])
                eng.drain()
                eng.tensor_mul(rows[0:1, 4, :], rows[0:1, 0, :], rows[0:1, 3, :])
                eng.drain()
                ch_ve.inc(eng.tensor_scalar(rows[0:1, 4, :], rows[0:1, 4, :], -1.0, None, ALU.mult))
                ch_mm.wait(eng, 68)
                tslot = 0
                for f in range(16):
                    if f >= 2:
                        ch_st.wait(eng, f - 2)
                    eng.tensor_mul(tmp[:], xT[:, f, :], ps_A[:])
                    eng.drain()
                    ch_ve.inc(eng.tensor_add(lnb[:, f % 2, :], tmp[:], ps_B[:]))
                    eng.drain()

            def e_gpsimd(eng):
                ch_st.wait_all(eng)
                ch_cc.inc(eng.collective_compute("AllGather", ALU.bypass, replica_groups=rg,
                                                 ins=[lnx_src[:]], outs=[gx[:]]))

            phase(b, {"scalar": e_scalar, "tensor": e_tensor, "vector": e_vector,
                      "gpsimd": e_gpsimd}, f"ln_{tag}")
            lctx.close()

        def emit_ffn(tag):
            lctx = ExitStack()
            cc_at = b.cnt.get("s_cc", 0)
            ch_gx = ChN(b, "f_gx", 1)
            ch_gx0 = ChN(b, "f_gx0", 4)
            ch_wg = ChN(b, "f_wg", 2)
            ch_wu = ChN(b, "f_wu", 2)
            ch_wd = ChN(b, "f_wd", 2)
            ch_mmg = Ch(b, "f_mmg")
            ch_gl = Ch(b, "f_gl")
            ch_int = Ch(b, "f_int")
            ch_mmd = Ch(b, "f_mmd")
            ch_dev = Ch(b, "f_dev")
            ch_pw = ChN(b, "f_pw", 2)
            ch_cc = Ch(b, "s_cc")
            ch_gd = ChN(b, "f_gd", 2)
            ch_va = Ch(b, "f_va")
            gxbuf = lctx.enter_context(nc.sbuf_tensor(f"ff_gx_{tag}", [128, NCH, TPC], BF16))
            inter = lctx.enter_context(nc.sbuf_tensor(f"ff_int_{tag}", [128, NCH, TPC], BF16))
            wg_t = lctx.enter_context(nc.sbuf_tensor(f"ff_wg_{tag}", [128, 2, NCH, 128], BF16))
            wu_t = lctx.enter_context(nc.sbuf_tensor(f"ff_wu_{tag}", [128, 2, NCH, 128], BF16))
            wd_t = lctx.enter_context(nc.sbuf_tensor(f"ff_wd_{tag}", [128, 2, NCH, 128], BF16))
            glbuf = lctx.enter_context(nc.sbuf_tensor(f"ff_gl_{tag}", [128, 2, TPC], F32))
            pstg = lctx.enter_context(nc.sbuf_tensor(f"ff_ps_{tag}", [128, 2, TPC], BF16))
            dbuf = lctx.enter_context(nc.sbuf_tensor(f"ff_db_{tag}", [128, 2, TPC], F32))
            psA_g = lctx.enter_context(nc.psum_tensor(f"ff_pAg_{tag}", [128, TPC], F32))
            psA_u = lctx.enter_context(nc.psum_tensor(f"ff_pAu_{tag}", [128, TPC], F32))
            psB_g = lctx.enter_context(nc.psum_tensor(f"ff_pBg_{tag}", [128, TPC], F32))
            psB_u = lctx.enter_context(nc.psum_tensor(f"ff_pBu_{tag}", [128, TPC], F32))
            ps_g = (psA_g, psB_g)
            ps_u = (psA_u, psB_u)

            def wg_load(eng, k):
                i = k % 16
                ch_wg.inc(eng.dma_start(out=wg_t[:, k % 2, :, :],
                                        in_=wgT[:, i * 128:(i + 1) * 128]
                                        .rearrange("(c p) n -> p c n", p=128)), k)

            def e_sync(eng):
                wg_load(eng, 0)
                wg_load(eng, 1)
                b.cc_wait(eng, cc_at)
                for q in range(2):
                    ch_gx0.inc(eng.dma_start(
                        out=gxbuf[:, q * 4:(q + 1) * 4, :],
                        in_=gx[0].rearrange("(c p) t -> p c t", p=128)
                        [:, q * 4:(q + 1) * 4, :]), q)
                for s in range(8):
                    for i in range(16):
                        k = s * 16 + i
                        if k < 2:
                            continue
                        ch_mmg.wait(eng, k - 1)
                        wg_load(eng, k)
                    if s < 7:
                        ch_mmg.wait(eng, 16 * (s + 1))
                        ch_gx.inc(eng.dma_start(out=gxbuf[:],
                                                in_=gx[s + 1]
                                                .rearrange("(c p) t -> p c t", p=128)), s)
                ch_wg.wait_all(eng)

            def e_tensor(eng):
                for s in range(8):
                    if s == 0:
                        for q in range(4):
                            ch_gx0.wait(eng, q)
                    else:
                        ch_gx.wait(eng, s - 1)
                        ch_dev.wait(eng, 16 * s)
                    for i in range(16):
                        k = s * 16 + i
                        ch_wg.wait(eng, k)
                        ch_wu.wait(eng, k)
                        if k >= 2:
                            ch_int.wait(eng, k - 1)
                        for tau in range(2):
                            for hc in range(16):
                                mi_ = eng.matmul(ps_g[k % 2][:, tau * 512:(tau + 1) * 512],
                                                 wg_t[:, k % 2, hc, :],
                                                 gxbuf[:, hc, tau * 512:(tau + 1) * 512],
                                                 start=(hc == 0), stop=(hc == 15))
                            for hc in range(16):
                                mi_ = eng.matmul(ps_u[k % 2][:, tau * 512:(tau + 1) * 512],
                                                 wu_t[:, k % 2, hc, :],
                                                 gxbuf[:, hc, tau * 512:(tau + 1) * 512],
                                                 start=(hc == 0), stop=(hc == 15))
                        ch_mmg.inc(mi_)
                    ch_int.wait(eng, 16 * (s + 1))
                    for j in range(16):
                        k = s * 16 + j
                        ch_wd.wait(eng, k)
                        if k >= 2:
                            ch_dev.wait(eng, k - 1)
                        for tau in range(2):
                            for ic in range(16):
                                mi_ = eng.matmul(ps_g[k % 2][:, tau * 512:(tau + 1) * 512],
                                                 wd_t[:, k % 2, ic, :],
                                                 inter[:, ic, tau * 512:(tau + 1) * 512],
                                                 start=(ic == 0), stop=(ic == 15))
                        ch_mmd.inc(mi_)

            def e_scalar(eng):
                def wd_load(k):
                    j = k % 16
                    s_ = k // 16
                    if k >= 2:
                        ch_mmd.wait(eng, k - 1)
                    ch_wd.inc(eng.dma_start(out=wd_t[:, k % 2, :, :],
                                            in_=wdT[:, j * 128:(j + 1) * 128]
                                            .rearrange("(c p) n -> p c n", p=128)), k)

                wd_load(0)
                wd_load(1)
                for s in range(8):
                    for i in range(16):
                        k = s * 16 + i
                        ch_mmg.wait(eng, k + 1)
                        if k >= 2:
                            ch_int.wait(eng, k - 1)
                        ch_gl.inc(eng.activation(glbuf[:, k % 2, :], ps_g[k % 2][:],
                                                 AF.Gelu_apprx_tanh))
                        if i % 8 == 0 and s * 16 + i // 8 + 2 < 128:
                            pass
                    for j in range(16):
                        k = s * 16 + j
                        if k + 2 < 128:
                            wd_load(k + 2)
                        ch_mmd.wait(eng, k + 1)
                        if k >= 2:
                            ch_pw.wait(eng, k - 2)
                        ch_dev.inc(eng.copy(pstg[:, k % 2, :], ps_g[k % 2][:]))
                        eng.drain()
                        ch_pw.inc(eng.dma_start(out=psrc[s, j * 128:(j + 1) * 128, :],
                                                in_=pstg[:, k % 2, :]), k)
                ch_pw.wait_all(eng)
                ch_wd.wait_all(eng)

            def e_vector(eng):
                for s in range(8):
                    if s >= 1:
                        ch_mmd.wait(eng, 16 * s)
                    for i in range(16):
                        k = s * 16 + i
                        ch_gl.wait(eng, k + 1)
                        ch_int.inc(eng.tensor_mul(inter[:, i, :], glbuf[:, k % 2, :],
                                                  ps_u[k % 2][:]))
                for f in range(16):
                    ch_gd.wait(eng, f)
                    ch_va.inc(eng.tensor_add(xT[:, f, :], xT[:, f, :], dbuf[:, f % 2, :]))

            def wu_load(eng, k):
                i = k % 16
                ch_wu.inc(eng.dma_start(out=wu_t[:, k % 2, :, :],
                                        in_=wuT[:, i * 128:(i + 1) * 128]
                                        .rearrange("(c p) n -> p c n", p=128)), k)

            def e_gpsimd(eng):
                wu_load(eng, 0)
                wu_load(eng, 1)
                b.cc_wait(eng, cc_at)
                for q in range(2, 4):
                    ch_gx0.inc(eng.dma_start(
                        out=gxbuf[:, q * 4:(q + 1) * 4, :],
                        in_=gx[0].rearrange("(c p) t -> p c t", p=128)
                        [:, q * 4:(q + 1) * 4, :]), q)
                for k in range(2, 128):
                    ch_mmg.wait(eng, k - 1)
                    wu_load(eng, k)
                ch_wu.wait_all(eng)
                ch_pw.wait_all(eng)
                ch_cc.inc(eng.collective_compute("ReduceScatter", ALU.add, replica_groups=rg,
                                                 ins=[psrc[:]], outs=[pdst[:]]))
                b.cc_wait(eng)
                for f in range(16):
                    if f >= 2:
                        ch_va.wait(eng, f - 1)
                    ch_gd.inc(eng.dma_start(out=dbuf[:, f % 2, :],
                                            in_=pdst[f * 128:(f + 1) * 128, :]), f)
                ch_gd.wait_all(eng)

            phase(b, {"sync": e_sync, "tensor": e_tensor, "scalar": e_scalar,
                      "vector": e_vector, "gpsimd": e_gpsimd}, f"ffn_{tag}")
            lctx.close()

        def emit_attn(tag):
            A = (tag == "attn_a")
            octx = ExitStack()
            if A:
                kvT = octx.enter_context(nc.sbuf_tensor("at_kvT", [128, 8192], BF16))
                vh = octx.enter_context(nc.sbuf_tensor("at_vh", [128, 64, 65], BF16))
            else:
                kT = octx.enter_context(nc.sbuf_tensor("bt_kT", [128, 2, 8192], BF16))
                vh = octx.enter_context(nc.sbuf_tensor("bt_vh", [128, 2, 64, 128], BF16))
            mctx = ExitStack()

            def proj_phase():
                lctx = ExitStack()
                cc_at = b.cnt.get("s_cc", 0)
                nob = 3 if A else 6
                ch_gx = ChN(b, "a_gx", 2)
                ch_wl = ChN(b, "a_wl", 1, per_item=(3 if A else 6))
                ch_mmp = Ch(b, "a_mmp")
                ch_pe = Ch(b, "a_pe")
                ch_rp = Ch(b, "a_rp")
                ch_qs = ChN(b, "a_qs", 2, per_item=2)
                ch_vs = ChN(b, "a_vs", 2, per_item=2)
                ch_vt = ChN(b, "a_vt", 2)
                ch_vhm = Ch(b, "a_vhm")
                ch_vhe = Ch(b, "a_vhe")
                gxb = lctx.enter_context(nc.sbuf_tensor(
                    f"a_gxb_{tag}", [128, 2, NCH, 512] if A else [128, NCH, 512], BF16))
                qb = lctx.enter_context(nc.sbuf_tensor(f"a_qb_{tag}", [128, 2, 2, 512], BF16))
                xsh = (lctx.enter_context(nc.sbuf_tensor(f"a_xsh_{tag}", [128, 512], BF16))
                       if A else None)
                ps_p = lctx.enter_context(nc.psum_tensor(f"a_psp_{tag}", [128, 4, 512], F32))
                ps_vt = lctx.enter_context(nc.psum_tensor(f"a_pvt_{tag}", [128, 2, 1024], BF16))
                vtt = lctx.enter_context(nc.sbuf_tensor(f"a_vtt_{tag}", [128, 2, 128], BF16))
                if A:
                    wsl = lctx.enter_context(nc.sbuf_tensor("a_wsl", [128, NCH, 384], BF16))
                    cosb = lctx.enter_context(nc.sbuf_tensor("a_cos", [128, 8192], BF16))
                    sinb = lctx.enter_context(nc.sbuf_tensor("a_sin", [128, 8192], BF16))
                else:
                    wsl = lctx.enter_context(nc.sbuf_tensor("b_wsl", [128, NCH, 768], BF16))
                    vb = lctx.enter_context(nc.sbuf_tensor("b_vb", [128, 2, 2, 512], BF16))
                    bqs = lctx.enter_context(nc.sbuf_tensor("b_bqs", [128, 6], F32))

                def e_gpsimd(eng):
                    if A:
                        ch_wl.inc(eng.dma_start(
                            out=wsl[:], in_=wqkv_sl.rearrange("(c p) o -> p c o", p=128)), 0)
                        ch_wl.inc(eng.dma_start(out=cosb[:], in_=cos_in[:]), 0)
                        ch_wl.inc(eng.dma_start(out=sinb[:], in_=sin_in[:]), 0)
                    else:
                        for wi, src in enumerate((wq_sl, wk_sl, wv_sl)):
                            ch_wl.inc(eng.dma_start(
                                out=wsl[:, :, wi * 256:(wi + 1) * 256],
                                in_=src.rearrange("(c p) o -> p c o", p=128)), 0)
                        for wi, src in enumerate((bq_in, bk_in, bv_in)):
                            ch_wl.inc(eng.dma_start(
                                out=bqs[:, wi * 2:(wi + 1) * 2],
                                in_=src.rearrange("c p -> p c"),
                                allow_slow_non_contiguous=True), 0)
                    ch_wl.wait_all(eng)

                vseq = [(hh_, 4 * t_ + r_) for t_ in range(16)
                        for hh_ in range(2) for r_ in range(4)]

                def e_scalar(eng):
                    def vcopies(tile):
                        for idx in range(8 * tile, 8 * tile + 8):
                            hh, kc = vseq[idx]
                            ch_vhm.wait(eng, idx + 1)
                            ch_vhe.inc(eng.copy(vh[:, hh, kc, :],
                                                ps_vt[:, idx % 2, 0:128]))

                    if A:
                        ch_wl.wait(eng, 0)
                        eng.activation(sinkx[:], sinkraw[:], AF.Exp, bias=colm20[0:1, 0:1])
                    for tt in range(16):
                        if A:
                            if tt >= 2:
                                ch_qs.wait(eng, tt - 2)
                            for ob in range(nob):
                                g = tt * nob + ob
                                ch_mmp.wait(eng, g + 1)
                                slot = g % 4
                                if ob < 2:
                                    ch_pe.inc(eng.activation(qb[:, tt % 2, ob, :],
                                                             ps_p[:, slot, :], AF.Copy,
                                                             scale=SCALE_A))
                                else:
                                    ch_pe.inc(eng.copy(kvT[:, tt * 512:(tt + 1) * 512],
                                                       ps_p[:, slot, :]))
                            ch_rp.wait(eng, tt + 1)
                            eng.drain()
                        else:
                            ch_pe.wait(eng, (tt + 1) * nob)
                        for cchunk in range(2):
                            ch_qs.inc(eng.dma_start(
                                out=qT_h[cchunk * 128:(cchunk + 1) * 128,
                                         tt * 512:(tt + 1) * 512],
                                in_=qb[:, tt % 2, cchunk, :]), tt)
                        if not A:
                            for hh in range(2):
                                ch_vs.inc(eng.dma_start(
                                    out=vT_h[hh * 128:(hh + 1) * 128, tt * 512:(tt + 1) * 512],
                                    in_=vb[:, tt % 2, hh, :]), tt)
                            if tt >= 2:
                                vcopies(tt - 2)
                    if not A:
                        vcopies(14)
                        vcopies(15)
                    if A:
                        for kc in range(64):
                            ch_vhm.wait(eng, kc + 1)
                            ch_vhe.inc(eng.copy(vh[:, kc, 0:64],
                                                ps_vt[:, kc % 2, 0:64]))
                    ch_qs.wait_all(eng)
                    ch_vs.wait_all(eng)

                def e_sync(eng):
                    def vloads(tile):
                        ch_vs.wait(eng, tile)
                        for idx in range(8 * tile, 8 * tile + 8):
                            hh, kc = vseq[idx]
                            if idx >= 2:
                                ch_vhm.wait(eng, idx - 1)
                            ch_vt.inc(eng.dma_start(
                                out=vtt[:, idx % 2, :],
                                in_=vT_h[hh * 128:(hh + 1) * 128,
                                         kc * 128:(kc + 1) * 128]), idx)

                    b.cc_wait(eng, cc_at)
                    for tt in range(16):
                        if A:
                            if tt >= 2:
                                ch_mmp.wait(eng, (tt - 1) * nob)
                        elif tt >= 1:
                            ch_mmp.wait(eng, tt * nob)
                        s_, half = tt // 2, tt % 2
                        ch_gx.inc(eng.dma_start(
                            out=gxb[:, tt % 2] if A else gxb[:],
                            in_=gx[s_].rearrange("(c p) t -> p c t", p=128)
                            [:, :, half * 512:(half + 1) * 512]), tt)
                        if (not A) and tt >= 2:
                            vloads(tt - 2)
                    if not A:
                        vloads(14)
                        vloads(15)

                def e_tensor(eng):
                    def vtrans(tile):
                        for idx in range(8 * tile, 8 * tile + 8):
                            ch_vt.wait(eng, idx)
                            if idx >= 2:
                                ch_vhe.wait(eng, idx - 1)
                            ch_vhm.inc(eng.matmul(ps_vt[:, idx % 2, 0:128],
                                                  vtt[:, idx % 2, :],
                                                  identb[:], start=True, stop=True,
                                                  is_transpose=True))

                    ch_wl.wait(eng, 0)
                    for tt in range(16):
                        ch_gx.wait(eng, tt)
                        for ob in range(nob):
                            g = tt * nob + ob
                            slot = g % 4
                            if g >= 4:
                                ch_pe.wait(eng, g - 3)
                            if A:
                                wcols = wsl[:, :, ob * 128:(ob + 1) * 128]
                            else:
                                wi, hh = ob // 2, ob % 2
                                wcols = wsl[:, :, wi * 256 + hh * 128: wi * 256 + (hh + 1) * 128]
                            gxs = gxb[:, tt % 2] if A else gxb[:]
                            for hc in range(16):
                                mi_ = eng.matmul(ps_p[:, slot, :], wcols[:, hc, :],
                                                 gxs[:, hc, :],
                                                 start=(hc == 0), stop=(hc == 15))
                            ch_mmp.inc(mi_)
                        if (not A) and tt >= 2:
                            vtrans(tt - 2)
                    if A:
                        ch_pe.wait(eng, 16 * nob)
                        for kc in range(64):
                            if kc >= 2:
                                ch_vhe.wait(eng, kc - 1)
                            ch_vhm.inc(eng.matmul(ps_vt[:, kc % 2, 0:64],
                                                  kvT[64:128, kc * 128:(kc + 1) * 128],
                                                  identb[64:128, 64:128], start=True,
                                                  stop=True, is_transpose=True))
                    else:
                        vtrans(14)
                        vtrans(15)

                def e_vector(eng):
                    if not A:
                        ch_wl.wait(eng, 0)
                        for tt in range(16):
                            if tt >= 2:
                                ch_qs.wait(eng, tt - 2)
                                ch_vs.wait(eng, tt - 2)
                            for ob in range(nob):
                                g = tt * nob + ob
                                ch_mmp.wait(eng, g + 1)
                                slot = g % 4
                                if ob < 2:
                                    ch_pe.inc(eng.tensor_scalar(
                                        qb[:, tt % 2, ob, :], ps_p[:, slot, :],
                                        bqs[:, ob:ob + 1], SCALE_B, ALU.add, ALU.mult))
                                elif ob < 4:
                                    hh = ob - 2
                                    ch_pe.inc(eng.tensor_scalar(
                                        kT[:, hh, tt * 512:(tt + 1) * 512],
                                        ps_p[:, slot, :], bqs[:, 2 + hh:3 + hh],
                                        None, ALU.add))
                                else:
                                    hh = ob - 4
                                    ch_pe.inc(eng.tensor_scalar(
                                        vb[:, tt % 2, hh, :], ps_p[:, slot, :],
                                        bqs[:, 4 + hh:5 + hh], None, ALU.add))
                        return
                    for tt in range(16):
                        base = tt * nob
                        ch_pe.wait(eng, base + 3)
                        if A:
                            if tt >= 2:
                                ch_qs.wait(eng, tt - 2)
                            cs = cosb[:, tt * 512:(tt + 1) * 512]
                            sn = sinb[:, tt * 512:(tt + 1) * 512]
                            for cchunk in range(2):
                                qap = qb[:, tt % 2, cchunk, :]
                                eng.drain()
                                for pb in (0, 64):
                                    eng.tensor_copy(xsh[pb:pb + 16, :], qap[pb + 32:pb + 48, :])
                                    eng.tensor_copy(xsh[pb + 32:pb + 48, :], qap[pb:pb + 16, :])
                                eng.drain()
                                for pb in (0, 32, 64, 96):
                                    eng.tensor_mul(qap[pb:pb + 16, :], qap[pb:pb + 16, :],
                                                   cs[pb:pb + 16, :])
                                    eng.tensor_mul(xsh[pb:pb + 16, :], xsh[pb:pb + 16, :],
                                                   sn[pb:pb + 16, :])
                                eng.drain()
                                for pb in (0, 32, 64, 96):
                                    eng.tensor_add(qap[pb:pb + 16, :], qap[pb:pb + 16, :],
                                                   xsh[pb:pb + 16, :])
                            kap = kvT[:, tt * 512:(tt + 1) * 512]
                            eng.drain()
                            eng.tensor_copy(xsh[0:16, :], kap[32:48, :])
                            eng.tensor_copy(xsh[32:48, :], kap[0:16, :])
                            eng.drain()
                            for pb in (0, 32):
                                eng.tensor_mul(kap[pb:pb + 16, :], kap[pb:pb + 16, :],
                                               cs[pb:pb + 16, :])
                                eng.tensor_mul(xsh[pb:pb + 16, :], xsh[pb:pb + 16, :],
                                               sn[pb:pb + 16, :])
                            eng.drain()
                            for pb in (0, 32):
                                eng.tensor_add(kap[pb:pb + 16, :], kap[pb:pb + 16, :],
                                               xsh[pb:pb + 16, :])
                        ch_rp.inc(eng.engine_nop())

                phase(b, {"gpsimd": e_gpsimd, "scalar": e_scalar, "sync": e_sync,
                          "tensor": e_tensor, "vector": e_vector}, f"{tag}_proj")
                lctx.close()

            def attn_phase():
                lctx = ExitStack()
                ch_qb = ChN(b, "a_qb", 2, per_item=(2 if A else 1))
                ch_tl = ChN(b, "a_tl", 1)
                ch_ms = Ch(b, "a_ms")
                ch_bm = Ch(b, "a_bm")
                ch_xp = Ch(b, "a_xp")
                ch_ma = Ch(b, "a_ma")
                ch_on = Ch(b, "a_on")
                ch_nf = Ch(b, "a_nf")
                ch_mn = Ch(b, "a_mn")
                ch_no = Ch(b, "a_no")
                qblk = lctx.enter_context(nc.sbuf_tensor(f"a_qblk_{tag}", [128, 2, 512], BF16))
                ptb = lctx.enter_context(nc.sbuf_tensor(f"a_pt_{tag}", [128, 4, 512], BF16))
                toepb = lctx.enter_context(nc.sbuf_tensor(f"a_toep_{tag}", [128, 1664], F32))
                tmpf = lctx.enter_context(nc.sbuf_tensor(f"a_tmpf_{tag}", [128, 2, 512], F32))
                d2r = lctx.enter_context(nc.sbuf_tensor(f"a_d2r_{tag}", [1, 2, 512], F32))
                nfr = lctx.enter_context(nc.sbuf_tensor(f"a_nfr_{tag}", [1, 2, 512], F32))
                drow = lctx.enter_context(nc.sbuf_tensor(f"a_dr_{tag}", [1, 512], F32))
                ps_s = lctx.enter_context(nc.psum_tensor(f"a_pss_{tag}", [128, 4, 512], F32))
                ps_oA = lctx.enter_context(nc.psum_tensor(f"a_poA_{tag}", [128, 512], F32))
                ps_oB = lctx.enter_context(nc.psum_tensor(f"a_poB_{tag}", [128, 512], F32))
                ps_n = lctx.enter_context(nc.psum_tensor(f"a_pn_{tag}", [128, 512], F32))
                ps_d = (lctx.enter_context(nc.psum_tensor(f"a_pd_{tag}", [1, 512], F32))
                        if not A else None)
                ps_o = (ps_oA, ps_oB)

                nvh = 64 if A else 128
                units = [(bb, m) for bb in range(2) for m in range(4)] if A \
                    else [(bb, hh) for bb in range(2) for hh in range(2)]
                iters = [(u, qb_) for u in units for qb_ in range(8)]
                nit = len(iters)
                band_counts = [band_js(qb_) if A else [] for (u, qb_) in iters]
                cum_band = []
                nb = 0
                for it in range(nit):
                    for j in range(32):
                        if j in band_counts[it]:
                            nb += 1
                        cum_band.append(nb)

                def e_sync(eng):
                    ch_tl.inc(eng.dma_start(out=toepb[:], in_=toep_in[:]), 0)
                    for it, (u, qb_) in enumerate(iters):
                        if it >= 2:
                            ch_ms.wait(eng, 32 * (it - 1))
                        rr = u[1] * (64 if A else 128)
                        bb = u[0]
                        src = qT_h[rr:rr + (64 if A else 128),
                                   bb * 4096 + qb_ * 512: bb * 4096 + (qb_ + 1) * 512]
                        ch_qb.inc(eng.dma_start(
                            out=qblk[0:(64 if A else 128), it % 2, :], in_=src), it)
                        if A:
                            ch_qb.inc(eng.dma_start(
                                out=qblk[64:128, it % 2, :], in_=src), it)

                on0 = 2 if A else 1

                def e_tensor(eng):
                    ch_on.wait(eng, on0)
                    for it, (u, qb_) in enumerate(iters):
                        bb, mh = u
                        ch_qb.wait(eng, it)
                        if it >= 2:
                            ch_on.wait(eng, on0 + 2 * (it - 1))
                        def av_pair(p):
                            ch_xp.wait(eng, it * 16 + p + 1)
                            for j in (2 * p, 2 * p + 1):
                                g = it * 32 + j
                                kcg = bb * 32 + j
                                av_l = vh[:, kcg, 0:65] if A else vh[:, mh, kcg, :]
                                mi_ = eng.matmul(ps_o[it % 2][0:(65 if A else 128), :], av_l,
                                                 ptb[:, g % 4, :],
                                                 start=(j == 0), stop=(j == 31))
                                if not A:
                                    mi_ = eng.matmul(ps_d[0:1, :], ones_cb[:],
                                                     ptb[:, g % 4, :],
                                                     start=(j == 0), stop=(j == 31),
                                                     skip_group_check=True)
                                ch_ma.inc(mi_)

                        for p in range(16):
                            if p == 6 and it >= 1:
                                itp = it - 1
                                ch_nf.wait(eng, itp + 1)
                                ch_no.wait(eng, itp)
                                ch_mn.inc(eng.matmul(ps_n[:], ones_r[0:1, 0:128],
                                                     nfr[0:1, itp % 2, :],
                                                     start=True, stop=True))
                            for j in (2 * p, 2 * p + 1):
                                g = it * 32 + j
                                if g >= 4:
                                    ch_xp.wait(eng, (g - 4) // 2 + 1)
                                kc0 = bb * 4096 + j * 128
                                if A:
                                    lo = 64 * (j & 1)
                                    lhs = kvT[lo:lo + 64, kc0:kc0 + 128]
                                    rhs = qblk[lo:lo + 64, it % 2, :]
                                else:
                                    lhs = kT[:, mh, kc0:kc0 + 128]
                                    rhs = qblk[:, it % 2, :]
                                ch_ms.inc(eng.matmul(ps_s[:, g % 4, :], lhs, rhs,
                                                     start=True, stop=True))
                            if p >= 2:
                                av_pair(p - 2)
                        av_pair(14)
                        av_pair(15)
                    itp = nit - 1
                    ch_nf.wait(eng, itp + 1)
                    ch_no.wait(eng, itp)
                    ch_mn.inc(eng.matmul(ps_n[:], ones_r[0:1, 0:128],
                                         nfr[0:1, itp % 2, :], start=True, stop=True))

                def e_scalar(eng):
                    for it, (u, qb_) in enumerate(iters):
                        for p in range(16):
                            g0 = it * 32 + 2 * p
                            ch_ms.wait(eng, g0 + 2)
                            if g0 >= 2:
                                ch_ma.wait(eng, g0 - 2)
                            if A and cum_band[g0 + 1] > 0:
                                ch_bm.wait(eng, cum_band[g0 + 1])
                            sl = (2 * p) % 4
                            ch_xp.inc(eng.activation(ptb[:, sl:sl + 2, :],
                                                     ps_s[:, sl:sl + 2, :],
                                                     AF.Exp, bias=colm20[:, 0:1]))
                        ch_ma.wait(eng, 32 * (it + 1))
                        if it >= 2:
                            ch_no.wait(eng, it - 1)
                        if A:
                            ch_on.inc(eng.copy(tmpf[0:64, it % 2, :], ps_o[it % 2][0:64, :]))
                            ch_on.inc(eng.copy(d2r[0:1, it % 2, :],
                                               ps_o[it % 2][64:65, :]))
                        else:
                            ch_on.inc(eng.copy(tmpf[:, it % 2, :], ps_o[it % 2][:]))
                            ch_on.inc(eng.copy(d2r[0:1, it % 2, :], ps_d[0:1, :]))

                def e_vector(eng):
                    if A:
                        ch_on.inc(eng.memset(vh[:, :, 64:65], 1.0))
                        eng.tensor_copy(kvT[64:128, 0:4096], kvT[0:64, 0:4096])
                        eng.drain()
                        ch_on.inc(eng.tensor_copy(kvT[64:128, 4096:8192],
                                                  kvT[0:64, 4096:8192]))
                        eng.drain()
                    else:
                        ch_on.inc(eng.engine_nop())
                    ch_tl.wait(eng, 0)

                    def epilogue(it):
                        u, qb_ = iters[it]
                        ch_on.wait(eng, on0 + 2 * (it + 1))
                        if A:
                            eng.tensor_single_scalar(drow[:], d2r[0:1, it % 2, :],
                                                     sinkx[0:1, u[1]:u[1] + 1], ALU.add)
                            eng.drain()
                            eng.reciprocal(drow[:], drow[:])
                        else:
                            eng.reciprocal(drow[:], d2r[0:1, it % 2, :])
                        eng.drain()
                        ch_nf.inc(eng.tensor_scalar(nfr[0:1, it % 2, :], drow[:],
                                                    1.0, None, ALU.mult))
                        ch_mn.wait(eng, it + 1)
                        bb = u[0]
                        col = bb * 4096 + qb_ * 512
                        if A:
                            mh = u[1]
                            dst = aout[(mh % 2) * 64:(mh % 2) * 64 + 64, mh // 2,
                                       col:col + 512]
                            ch_no.inc(eng.tensor_mul(dst, tmpf[0:64, it % 2, :],
                                                     ps_n[0:64, :]))
                        else:
                            hh = u[1]
                            dst = aout[:, hh, col:col + 512]
                            ch_no.inc(eng.tensor_mul(dst, tmpf[:, it % 2, :],
                                                     ps_n[:]))

                    def band_tt(it, qb_, j):
                        g = it * 32 + j
                        ch_ms.wait(eng, g + 1)
                        d0 = 512 * qb_ - 128 * j
                        ch_bm.inc(eng.tensor_add(ps_s[:, g % 4, :], ps_s[:, g % 4, :],
                                                 toepb[:, d0 + 511:d0 + 1023]))

                    for it, (u, qb_) in enumerate(iters):
                        bl = band_counts[it]
                        for j in bl:
                            if j < 8:
                                band_tt(it, qb_, j)
                        if it >= 1:
                            epilogue(it - 1)
                        for j in bl:
                            if j >= 8:
                                band_tt(it, qb_, j)
                    epilogue(nit - 1)

                def e_gpsimd(eng):
                    ch_no.wait(eng, nit)

                phase(b, {"sync": e_sync, "tensor": e_tensor, "scalar": e_scalar,
                          "vector": e_vector, "gpsimd": e_gpsimd}, f"{tag}_attn")
                lctx.close()

            def aproj_phase():
                lctx = ExitStack()
                ch_wl = ChN(b, "ap_wl", 1)
                ch_mw = Ch(b, "ap_mw")
                ch_ev = Ch(b, "ap_ev")
                ch_pw = ChN(b, "ap_pw", 2)
                ch_cc = Ch(b, "s_cc")
                ch_gd = ChN(b, "ap_gd", 2)
                ch_va = Ch(b, "ap_va")
                ch_bo = ChN(b, "ap_bo", 1)
                wcb = lctx.enter_context(nc.sbuf_tensor(f"ap_wcb_{tag}", [128, 2, D_], BF16))
                pstg = lctx.enter_context(nc.sbuf_tensor(f"ap_ps_{tag}", [128, 2, 512], BF16))
                dbuf = lctx.enter_context(nc.sbuf_tensor(f"ap_db_{tag}", [128, 2, TPC], F32))
                bob = lctx.enter_context(nc.sbuf_tensor(f"ap_bo_{tag}", [128, 16], F32))
                ps_w = lctx.enter_context(nc.psum_tensor(f"ap_psw_{tag}", [128, 4, 512], F32))
                wsrc = wout_sl if A else wo_sl

                def e_gpsimd(eng):
                    ch_wl.inc(eng.dma_start(
                        out=wcb[:], in_=wsrc.rearrange("(c p) o -> p c o", p=128)), 0)
                    ch_wl.wait_all(eng)
                    ch_pw.wait_all(eng)
                    ch_cc.inc(eng.collective_compute("ReduceScatter", ALU.add,
                                                     replica_groups=rg,
                                                     ins=[psrc_a[:]], outs=[pdst_a[:]]))
                    b.cc_wait(eng)
                    for f in range(16):
                        if f >= 2:
                            ch_va.wait(eng, f - 1)
                        ch_gd.inc(eng.dma_start(out=dbuf[:, f % 2, :],
                                                in_=pdst_a[f * 128:(f + 1) * 128, :]), f)
                    ch_gd.wait_all(eng)

                def e_tensor(eng):
                    ch_wl.wait(eng, 0)
                    for g in range(256):
                        oc, tt = g // 16, g % 16
                        if g >= 4:
                            ch_ev.wait(eng, g - 3)
                        for fc in range(2):
                            mi_ = eng.matmul(ps_w[:, g % 4, :],
                                             wcb[:, fc, oc * 128:(oc + 1) * 128],
                                             aout[:, fc, tt * 512:(tt + 1) * 512],
                                             start=(fc == 0), stop=(fc == 1))
                        ch_mw.inc(mi_)

                def e_scalar(eng):
                    for g in range(256):
                        oc, tt = g // 16, g % 16
                        ch_mw.wait(eng, g + 1)
                        if g >= 2:
                            ch_pw.wait(eng, g - 2)
                        ch_ev.inc(eng.copy(pstg[:, g % 2, :], ps_w[:, g % 4, :]))
                        eng.drain()
                        ch_pw.inc(eng.dma_start(
                            out=psrc_a[tt // 2, oc * 128:(oc + 1) * 128,
                                       (tt % 2) * 512:(tt % 2) * 512 + 512],
                            in_=pstg[:, g % 2, :]), g)
                    ch_pw.wait_all(eng)

                def e_sync(eng):
                    if not A:
                        ch_bo.inc(eng.dma_start(
                            out=bob[:], in_=bo_in.rearrange("c p -> p c"),
                            allow_slow_non_contiguous=True), 0)
                        ch_bo.wait_all(eng)

                def e_vector(eng):
                    for f in range(16):
                        ch_gd.wait(eng, f)
                        ch_va.inc(eng.tensor_add(xT[:, f, :], xT[:, f, :],
                                                 dbuf[:, f % 2, :]))

                def e_scalar_post(eng):
                    pass

                emitters = {"gpsimd": e_gpsimd, "tensor": e_tensor, "scalar": e_scalar,
                            "sync": e_sync, "vector": e_vector}
                phase(b, emitters, f"{tag}_aproj")

                if not A:
                    def e_vec_b(eng):
                        for f in range(16):
                            eng.tensor_scalar(xT[:, f, :], xT[:, f, :],
                                              bob[:, f:f + 1], None, ALU.add)

                    phase(b, {"vector": e_vec_b}, f"{tag}_bias")
                lctx.close()

            proj_phase()
            aout = mctx.enter_context(nc.sbuf_tensor(f"a_ao_{tag}", [128, 2, 8192], BF16))
            attn_phase()
            if DEBUG_ATTN:
                dq_ = nc.dram_tensor(f"dbg_qT_{tag}", [256, 8192], BF16,
                                     kind="ExternalOutput").ap()
                dk_ = nc.dram_tensor(f"dbg_kv_{tag}", [128, 8192], BF16,
                                     kind="ExternalOutput").ap()
                da_ = nc.dram_tensor(f"dbg_ao_{tag}", [128, 2, 8192], BF16,
                                     kind="ExternalOutput").ap()
                dv_ = nc.dram_tensor(f"dbg_vh_{tag}", [128, 64, 65], BF16,
                                     kind="ExternalOutput").ap()

                def pdbg():
                    ch = ChN(b, "dbg_a", 1, per_item=4)

                    def e_sync(eng):
                        ch.inc(eng.dma_start(out=dq_[:], in_=qT_h[:]), 0)
                        if A:
                            ch.inc(eng.dma_start(out=dk_[:], in_=kvT[:]), 0)
                            ch.inc(eng.dma_start(out=dv_[:], in_=vh[:]), 0)
                        else:
                            ch.inc(eng.dma_start(out=dk_[:], in_=kT[:, 0, :]), 0)
                            ch.inc(eng.dma_start(out=dv_[:], in_=vh[:, 0, :, 0:65]), 0)
                        ch.inc(eng.dma_start(out=da_[:], in_=aout[:]), 0)
                        ch.wait_all(eng)

                    phase(b, {"sync": e_sync}, f"{tag}_dbg")

                pdbg()
            aproj_phase()
            mctx.close()
            octx.close()


        for sub in ORDER:
            if sub in ("ffn1", "ffn2"):
                emit_ln(sub)
                emit_ffn(sub)
            else:
                emit_ln(sub)
                emit_attn(sub)

        def p9():
            lctx = ExitStack()
            ch_mm = Ch(b, "s_mm")
            ch_ev = Ch(b, "s_ve")
            ch_st = ChN(b, "p9_st", 2)
            ybuf = lctx.enter_context(nc.sbuf_tensor("p9_ybuf", [128, 2, D_], F32))
            pT = lctx.enter_context(nc.psum_tensor("p9_pT", [128, 2, 512], F32))

            def e_tensor(eng):
                for t in range(8):
                    for f in range(16):
                        k = t * 16 + f
                        if k >= 2:
                            ch_ev.wait(eng, k - 1)
                        ch_mm.inc(eng.matmul(pT[:, k % 2, 0:128], xT[:, f, t * 128:(t + 1) * 128],
                                             identf[:], start=True, stop=True,
                                             is_transpose=True))

            def e_vector(eng):
                for t in range(8):
                    if t >= 2:
                        ch_st.wait(eng, t - 2)
                    for f in range(16):
                        k = t * 16 + f
                        ch_mm.wait(eng, k + 1)
                        ch_ev.inc(eng.tensor_copy(ybuf[:, t % 2, f * 128:(f + 1) * 128],
                                                  pT[:, k % 2, 0:128]))

            def e_sync(eng):
                for t in range(8):
                    ch_ev.wait(eng, 16 * (t + 1))
                    ch_st.inc(eng.dma_start(out=y_out[t * 128:(t + 1) * 128, :],
                                            in_=ybuf[:, t % 2, :]), t)
                ch_st.wait_all(eng)

            phase(b, {"tensor": e_tensor, "vector": e_vector, "sync": e_sync}, "p9")
            lctx.close()

        p9()

        return nc


def kernel(**inputs):
    in_maps = stage_inputs(inputs)
    nc = build_program()
    res = run_bass_kernel_spmd(nc, in_maps, core_ids=list(range(NCORE)))
    out = np.empty((B_, L_, D_), np.float32)
    for c in range(NCORE):
        bb, r = c // 4, c % 4
        out[bb, r * TPC:(r + 1) * TPC, :] = np.asarray(res.results[c]["y"], np.float32)
    return out
